# version 11
# baseline (speedup 1.0000x reference)
import contextlib
import numpy as np
import ml_dtypes

import concourse.bass as bass
import concourse.mybir as mybir
from concourse.bass_utils import run_bass_kernel_spmd

F32 = mybir.dt.float32
BF16 = mybir.dt.bfloat16
AF = mybir.ActivationFunctionType
ALU = mybir.AluOpType

B, N, D, H, DK = 16, 2048, 1024, 16, 64
DH = 4096
BPC = 2
NCORE = 8
EPS = 1e-5

_CFG = {"mode": "full"}


class Sem:
    def __init__(self, nc, es, name, step=1):
        self.h = es.enter_context(nc.semaphore(name))
        self.n = 0
        self.step = step

    def tok(self):
        return (self, self.n)


def winc(sem, ins):
    ins.then_inc(sem.h, sem.step)
    sem.n += sem.step


def _aslist(t):
    return t if isinstance(t, list) else [t]


def slot_dma(eng, slot_sem, gates, out, in_):
    wait_toks(eng, gates + [slot_sem.tok()])
    winc(slot_sem, eng.dma_start(out=out, in_=in_))
    return slot_sem.tok()


def wait_toks(eng, toks):
    best = {}
    for t in toks:
        if t is None:
            continue
        s, v = t
        if v <= 0:
            continue
        if best.get(id(s), (None, -1))[1] < v:
            best[id(s)] = (s, v)
    for s, v in best.values():
        eng.wait_ge(s.h, v)


def build_program(mode="full"):
    nc = bass.Bass("TRN2", target_bir_lowering=False)

    dt_in = lambda n_, sh_: nc.dram_tensor(n_, sh_, F32, kind="ExternalInput").ap()
    xT = dt_in("xT", [BPC, D, N])
    wqd = dt_in("wq", [8, 128, D])
    wkd = dt_in("wk", [8, 128, D])
    wvd = dt_in("wv", [2, 128, 8 * 512])
    wod = dt_in("wo", [8, 128, D])
    w1d = dt_in("w1", [32, 128, D])
    w2d = dt_in("w2", [32, 128, D])
    h1d = dt_in("h1w", [32, 128, DH])
    h2d = dt_in("h2w", [32, 128, DH])
    w3d = dt_in("w3", [8, 128, DH])
    s1d = dt_in("s1", [128, 32])
    s2d = dt_in("s2", [128, 32])
    sh1d = dt_in("sh1", [128, 32])
    sh2d = dt_in("sh2", [128, 32])
    s3d = dt_in("s3", [128, 8])
    t1d = dt_in("t1", [128, N])
    t2d = dt_in("t2", [128, N])
    shd = dt_in("sh", [128, 256])
    trid = dt_in("tri", [128, 128])
    y = nc.dram_tensor("y", [BPC, D, N], F32, kind="ExternalOutput").ap()
    x1s = nc.dram_tensor("x1s", [BPC, D, N], F32, kind="Internal").ap()

    es = contextlib.ExitStack()
    sb = lambda n_, sh_, dt: es.enter_context(nc.sbuf_tensor(n_, sh_, dt))

    shA = sb("shA", [128, 128], BF16)
    shB = sb("shB", [128, 128], BF16)
    tri = sb("tri_c", [128, 128], BF16)
    ones_bf = sb("ones_bf", [128, 128], BF16)
    ones_f = sb("ones_f", [1, 128], F32)
    srow = sb("srow", [1, 5 * 512], F32)
    s1v = sb("s1v", [128, 32], F32)
    s2v = sb("s2v", [128, 32], F32)
    sh1v = sb("sh1v", [128, 32], F32)
    sh2v = sb("sh2v", [128, 32], F32)
    s3v = sb("s3v", [128, 8], F32)

    psA = es.enter_context(nc.psum_tensor("psA", [128, 1024], F32))
    psB = es.enter_context(nc.psum_tensor("psB", [128, 1024], F32))
    psC = es.enter_context(nc.psum_tensor("psC", [128, 1024], F32))
    psD = es.enter_context(nc.psum_tensor("psD", [128, 1024], F32))

    sLc = Sem(nc, es, "sLc", 16)
    sLt = Sem(nc, es, "sLt", 16)
    sLxt = [Sem(nc, es, f"sLxt{i}", 16) for i in range(4)]
    sLxr = [Sem(nc, es, f"sLxr{i}", 16) for i in range(2)]
    sLx1l = [Sem(nc, es, f"sLx1l{i}", 16) for i in range(2)]
    sLx1w = [Sem(nc, es, f"sLx1w{i}", 16) for i in range(2)]
    sSxr = [Sem(nc, es, f"sSxr{i}", 16) for i in range(2)]
    sSo = [Sem(nc, es, f"sSo{i}", 16) for i in range(2)]
    sCW = Sem(nc, es, "sCW", 16)
    sGx = [Sem(nc, es, f"sGx{i}", 16) for i in range(4)]
    sWba = [Sem(nc, es, f"sWba{i}", 16) for i in range(2)]
    sWv = Sem(nc, es, "sWv", 16)
    sW1 = [Sem(nc, es, f"sW1{i}", 16) for i in range(2)]
    sW2 = [Sem(nc, es, f"sW2{i}", 16) for i in range(2)]
    sWh = [Sem(nc, es, f"sWh{i}", 16) for i in range(2)]
    sPE = Sem(nc, es, "sPE")
    sSC = Sem(nc, es, "sSC")
    sDV = Sem(nc, es, "sDV")

    blk = es.enter_context(nc.Block())

    def _(sy):
        for dst, src in [(s1v, s1d[:, :]), (s2v, s2d[:, :]),
                         (sh1v, sh1d[:, :]), (sh2v, sh2d[:, :]), (s3v, s3d[:, :])]:
            winc(sLc, sy.dma_start(out=dst[:, :], in_=src))
    blk.sync(_)
    consts_ld = sLc.tok()

    def _(g):
        for dst, src in [(shA, shd[:, 0:128]), (shB, shd[:, 128:256]),
                         (tri, trid[:, :])]:
            winc(sCW, g.dma_start(out=dst[:, :], in_=src))
    blk.gpsimd(_)
    consts_w = sCW.tok()

    def _(v):
        v.memset(ones_bf[:, :], 1.0)
        winc(sDV, v.memset(ones_f[:, :], 1.0))
    blk.vector(_)
    consts_dv = sDV.tok()

    st = {}

    def emit_ln_stats(src_bf, sqb, st_pair, extra_pe_toks, extra_dv_toks,
                      ld_toks, roff):
        pe_base = sPE.n
        dv_toks = []
        def _(v):
            wait_toks(v, extra_dv_toks)
            for ic in range(8):
                w = [ld_toks[ic]]
                if ic >= 2:
                    w.append((sPE, pe_base + ic - 1))
                wait_toks(v, w)
                winc(sDV, v.tensor_mul(sqb[ic % 2][:, :], src_bf(ic), src_bf(ic)))
                dv_toks.append(sDV.tok())
        blk.vector(_)

        def _(t):
            wait_toks(t, [consts_dv] + extra_pe_toks)
            for ic in range(8):
                wait_toks(t, [dv_toks[ic]])
                t.matmul(st_pair[0:1, 0:512], ones_bf[:, 0:1], src_bf(ic),
                         start=(ic == 0), stop=(ic == 7))
                winc(sPE, t.matmul(st_pair[0:1, 512:1024], ones_bf[:, 0:1], sqb[ic % 2][:, :],
                                   start=(ic == 0), stop=(ic == 7)))
        blk.tensor(_)
        assert sPE.n == pe_base + 8
        stats_mm = sPE.tok()
        cons_toks = [[(sPE, pe_base + ic + 1), dv_toks[ic]] for ic in range(8)]

        def _(v):
            wait_toks(v, [stats_mm])
            v.tensor_scalar_mul(srow[:, roff:roff + 512], st_pair[0:1, 0:512], 1.0 / D)
            winc(sDV, v.tensor_scalar_mul(srow[:, roff + 512:roff + 1024],
                                          st_pair[0:1, 512:1024], 1.0 / D))
            v.wait_ge(sDV.h, sDV.n)
            winc(sDV, v.tensor_mul(srow[:, 2048:2560],
                                   srow[:, roff:roff + 512], srow[:, roff:roff + 512]))
            v.wait_ge(sDV.h, sDV.n)
            winc(sDV, v.tensor_sub(srow[:, roff + 512:roff + 1024],
                                   srow[:, roff + 512:roff + 1024],
                                   srow[:, 2048:2560]))
            v.wait_ge(sDV.h, sDV.n)
            winc(sDV, v.tensor_scalar_add(srow[:, roff + 512:roff + 1024],
                                          srow[:, roff + 512:roff + 1024], EPS))
        blk.vector(_)
        rows_tok = sDV.tok()

        def _(sc):
            wait_toks(sc, [rows_tok])
            winc(sSC, sc.activation(out=srow[:, roff + 512:roff + 1024],
                                    in_=srow[:, roff + 512:roff + 1024], func=AF.Ln))
            sc.wait_ge(sSC.h, sSC.n)
            winc(sSC, sc.activation(out=srow[:, roff + 512:roff + 1024],
                                    in_=srow[:, roff + 512:roff + 1024],
                                    func=AF.Exp, scale=-0.5))
        blk.scalar(_)
        inv_tok = sSC.tok()
        return inv_tok, rows_tok, cons_toks

    def emit_ln_bc(bc_pair, inv_tok, bc_free, roff):
        def _(t):
            wait_toks(t, [inv_tok] + bc_free)
            t.matmul(bc_pair[:, 0:512], ones_f[0:1, :], srow[0:1, roff:roff + 512],
                     start=True, stop=True)
            winc(sPE, t.matmul(bc_pair[:, 512:1024], ones_f[0:1, :],
                               srow[0:1, roff + 512:roff + 1024], start=True, stop=True))
        blk.tensor(_)
        return sPE.tok()

    def build_attention():
        aes = contextlib.ExitStack()
        asb = lambda n_, sh_, dt: aes.enter_context(nc.sbuf_tensor(n_, sh_, dt))
        t1 = asb("t1s", [128, N], F32)
        t2 = asb("t2s", [128, N], F32)
        act = asb("act", [128, 8 * N], BF16)
        att = asb("att", [128, 8 * N], BF16)
        qhp = asb("qhp", [128, N], BF16)
        khp = asb("khp", [128, N], BF16)
        vs = asb("vs", [128, 16 * 1024], BF16)
        p0 = asb("p0", [128, 16 * 512], BF16)
        p1 = asb("p1", [128, 16 * 512], BF16)
        xb_t = [asb(f"xbt{i}", [128, 512], BF16) for i in range(4)]
        wvs = asb("wvs", [128, 8 * 512], BF16)
        wba = [asb(f"wba{i}", [128, 8 * 128], BF16) for i in range(2)]
        sq_t = [asb(f"sqt{i}", [128, 512], BF16) for i in range(2)]
        u_sb = asb("u_sb", [128, 4 * 512], BF16)
        v_sb = asb("v_sb", [128, 4 * 512], BF16)
        u2_sb = asb("u2_sb", [128, 4 * 512], BF16)
        v2_sb = asb("v2_sb", [128, 4 * 512], BF16)
        scr = asb("scr", [128, 512], F32)
        scr2 = scr
        xre = [asb(f"xre{i}", [128, 512], F32) for i in range(2)]

        def _(sy):
            winc(sLt, sy.dma_start(out=t1[:, :], in_=t1d[:, :]))
            winc(sLt, sy.dma_start(out=t2[:, :], in_=t2d[:, :]))
        blk.sync(_)
        t12_tok = sLt.tok()

        pj = [psA[:, 0:512], psA[:, 512:1024], psB[:, 0:512], psB[:, 512:1024]]
        rp = [psC[:, 0:512], psC[:, 512:1024], psD[:, 0:512], psD[:, 512:1024]]

        st.update(bank_pj=[None] * 4, bank_pk=[None, None], wba_free=[None, None],
                  xt_free=[None, None, None, None], stats_free=[], uv_free=None, uv2_free=None,
                  act_free=None, qk_free=None, vs_free=None, p_free=None,
                  xre_free=[None, None])

        for b in range(BPC):
            ln_state = {'stats_free_e': st['stats_free'] + st['bank_pk'],
                        'stats_free_o': st['stats_free'] + st['bank_pk'],
                        'bc_free_e': st['bank_pj'], 'bc_free_o': st['bank_pj']}
            pend = [None]

            def ln1_flush(b):
                nch, ev, bcp, inv_tok, cons1, roff = pend[0]
                bc_tok = emit_ln_bc(bcp, inv_tok,
                                    ln_state['bc_free_e' if ev else 'bc_free_o'], roff)
                xldt2 = []
                dv_base = sDV.n
                def _(sy, nch=nch):
                    for ic in range(8):
                        g_ = list(cons1[6 + (ic % 2)]) + _aslist(st['xre_free'][ic % 2])
                        if ic >= 2:
                            g_.append((sDV, dv_base + 2 * ic - 2))
                        slot_dma(sy, sLxr[ic % 2], g_,
                                 xre[ic % 2][:, :],
                                 xT[b, ic * 128:(ic + 1) * 128, nch * 512:(nch + 1) * 512])
                        xldt2.append(sLxr[ic % 2].tok())
                blk.sync(_)

                norm_dv = sDV.n
                st.setdefault('ln_norm_dv', {})[nch] = norm_dv
                def _(v, nch=nch, bcp=bcp):
                    wait_toks(v, [bc_tok, st['act_free']])
                    for ic in range(8):
                        wait_toks(v, [xldt2[ic]])
                        v.wait_ge(sDV.h, sDV.n)
                        winc(sDV, v.tensor_sub(scr[:, :], xre[ic % 2][:, :], bcp[:, 0:512]))
                        v.wait_ge(sDV.h, sDV.n)
                        winc(sDV, v.tensor_mul(
                            act[:, ic * N + nch * 512: ic * N + nch * 512 + 512],
                            scr[:, :], bcp[:, 512:1024]))
                blk.vector(_)
                st['xre_free'] = [sDV.tok(), sDV.tok()]
                if ev:
                    ln_state['bc_free_e'] = [sDV.tok()]
                else:
                    ln_state['bc_free_o'] = [sDV.tok()]
                pend[0] = None

            for nch in range(4):
                ev = nch % 2 == 0
                stp = psD if ev else psC
                bcp = psA if ev else psB
                xldt = []
                pe_base0 = sPE.n
                dv_base0 = sDV.n
                def _(g, b=b, nch=nch):
                    for ic in range(8):
                        g_ = _aslist(st['xt_free'][ic % 4]) + [consts_ld]
                        if ic >= 4:
                            g_ += [(sPE, pe_base0 + ic - 3), (sDV, dv_base0 + ic - 3)]
                        slot_dma(g, sGx[ic % 4], g_,
                                 xb_t[ic % 4][:, :],
                                 xT[b, ic * 128:(ic + 1) * 128, nch * 512:(nch + 1) * 512])
                        xldt.append(sGx[ic % 4].tok())
                blk.gpsimd(_)

                inv_tok, rows_tok, cons1 = emit_ln_stats(
                    lambda ic: xb_t[ic % 4][:, :], sq_t, stp,
                    ln_state['stats_free_e' if ev else 'stats_free_o'], [],
                    xldt, (nch % 2) * 1024)
                if ev:
                    ln_state['stats_free_e'] = [rows_tok]
                else:
                    ln_state['stats_free_o'] = [rows_tok]
                st['xt_free'] = [cons1[4], cons1[5], cons1[6], cons1[7]]

                if pend[0] is not None:
                    ln1_flush(b)
                pend[0] = (nch, ev, bcp, inv_tok, cons1, (nch % 2) * 1024)
            ln1_flush(b)
            xln_tok = sDV.tok()
            st['stats_free'] = ln_state['stats_free_e']
            st['bank_pk'] = [xln_tok]
            st['bank_pj'] = [xln_tok] * 4

            vevac = []
            vsc0 = sSC.n
            for dvc in range(2):
                def _(g, dvc=dvc):
                    g_ = [(sSC, vsc0 + 16)] if dvc == 1 else []
                    slot_dma(g, sWv, g_, wvs[:, :], wvd[dvc])
                blk.gpsimd(_)
                wv_tok = sWv.tok()

                vpe0 = sPE.n
                def _(t, dvc=dvc, wv_tok=wv_tok):
                    for nt in range(16):
                        gi = dvc * 16 + nt
                        toks = [wv_tok, xln_tok]
                        if gi < 4:
                            toks += st['bank_pj']
                        else:
                            toks.append((sSC, vsc0 + gi - 3))
                        wait_toks(t, toks)
                        ins = None
                        for ic in range(8):
                            ins = t.matmul(pj[gi % 4],
                                           act[:, ic * N + nt * 128: ic * N + nt * 128 + 128],
                                           wvs[:, ic * 512:(ic + 1) * 512],
                                           start=(ic == 0), stop=(ic == 7))
                        winc(sPE, ins)
                blk.tensor(_)

                def _(sc, dvc=dvc, vpe0=vpe0):
                    for nt in range(16):
                        gi = dvc * 16 + nt
                        wait_toks(sc, [(sPE, vpe0 + nt + 1), st['vs_free']])
                        winc(sSC, sc.copy(vs[:, nt * 1024 + dvc * 512: nt * 1024 + dvc * 512 + 512],
                                          pj[gi % 4]))
                blk.scalar(_)
            vevac = [(sSC, vsc0 + k + 1) for k in range(32)]
            st['bank_pj'] = [sSC.tok()] * 4
            v_done = sSC.tok()

            def score_mm(t, tt_, hp, c):
                t.matmul(psA[:, (tt_ % 2) * 512:(tt_ % 2) * 512 + 512],
                         khp[0:64, tt_ * 128:(tt_ + 1) * 128],
                         qhp[0:64, c * 512:(c + 1) * 512],
                         start=True, stop=True)
                return t.matmul(psB[:, (tt_ % 2) * 512:(tt_ % 2) * 512 + 512],
                                khp[64:128, tt_ * 128:(tt_ + 1) * 128],
                                qhp[64:128, c * 512:(c + 1) * 512],
                                start=True, stop=True, tile_position=(64, 0))

            def emit_av_pair(t, ta, tb, hp, ntl):
                for tt_ in (ta, tb):
                    t.matmul(psC[0:64, 0:512],
                             vs[:, tt_ * 1024 + (2 * hp) * 64: tt_ * 1024 + (2 * hp) * 64 + 64],
                             p0[:, tt_ * 512:(tt_ + 1) * 512],
                             start=(tt_ == 0), stop=(tt_ == ntl - 1), tile_position=(0, 0))
                    t.matmul(psC[64:128, 512:1024],
                             vs[:, tt_ * 1024 + (2 * hp + 1) * 64: tt_ * 1024 + (2 * hp + 1) * 64 + 64],
                             p1[:, tt_ * 512:(tt_ + 1) * 512],
                             start=(tt_ == 0), stop=(tt_ == ntl - 1), tile_position=(0, 64))
                ins = None
                for tt_ in (ta, tb):
                    t.matmul(psD[0:64, 0:512], ones_bf[:, 0:64], p0[:, tt_ * 512:(tt_ + 1) * 512],
                             start=(tt_ == 0), stop=(tt_ == ntl - 1), tile_position=(0, 0))
                    ins = t.matmul(psD[64:128, 512:1024], ones_bf[:, 0:64],
                                   p1[:, tt_ * 512:(tt_ + 1) * 512],
                                   start=(tt_ == 0), stop=(tt_ == ntl - 1), tile_position=(0, 64))
                return ins

            for hp in range(8):
                pk = [psC[:, 0:512], psC[:, 512:1024], psD[:, 0:512], psD[:, 512:1024]]
                def _(g, hp=hp):
                    slot_dma(g, sWba[0], [st['wba_free'][0]], wba[0][:, :], wqd[hp])
                    slot_dma(g, sWba[1], [st['wba_free'][1]], wba[1][:, :], wkd[hp])
                blk.gpsimd(_)
                wtq, wtk = sWba[0].tok(), sWba[1].tok()

                peq = sPE.n
                def _(t, hp=hp):
                    wait_toks(t, [wtq] + st['bank_pj'])
                    for ic in range(8):
                        for nn in range(4):
                            if hp == 0:
                                wait_toks(t, [(sDV, st['ln_norm_dv'][nn] + 2 * ic + 2)])
                            elif ic == 0 and nn == 0:
                                wait_toks(t, [xln_tok])
                            ins = t.matmul(pj[nn], wba[0][:, ic * 128:(ic + 1) * 128],
                                           act[:, ic * N + nn * 512: ic * N + nn * 512 + 512],
                                           start=(ic == 0), stop=(ic == 7))
                            if ic == 7:
                                winc(sPE, ins)
                    wait_toks(t, [wtk] + st['bank_pk'])
                    for ic in range(8):
                        for nn in range(4):
                            ins = t.matmul(pk[nn], wba[1][:, ic * 128:(ic + 1) * 128],
                                           act[:, ic * N + nn * 512: ic * N + nn * 512 + 512],
                                           start=(ic == 0), stop=(ic == 7))
                            if ic == 7:
                                winc(sPE, ins)
                blk.tensor(_)
                pek = peq + 4
                st['wba_free'] = [sPE.tok(), sPE.tok()]

                dvq = sDV.n
                def _(v):
                    wait_toks(v, [st['uv_free'], t12_tok])
                    for nn in range(4):
                        v.wait_ge(sPE.h, peq + nn + 1)
                        v.tensor_mul(u_sb[:, nn * 512:(nn + 1) * 512], pj[nn],
                                     t1[:, nn * 512:(nn + 1) * 512])
                        winc(sDV, v.tensor_mul(v_sb[:, nn * 512:(nn + 1) * 512], pj[nn],
                                               t2[:, nn * 512:(nn + 1) * 512]))
                    wait_toks(v, [st['uv2_free']])
                    for nn in range(4):
                        v.wait_ge(sPE.h, pek + nn + 1)
                        v.tensor_mul(u2_sb[:, nn * 512:(nn + 1) * 512], pk[nn],
                                     t1[:, nn * 512:(nn + 1) * 512])
                        winc(sDV, v.tensor_mul(v2_sb[:, nn * 512:(nn + 1) * 512], pk[nn],
                                               t2[:, nn * 512:(nn + 1) * 512]))
                blk.vector(_)
                dvk = dvq + 4

                rpq = sPE.n
                def _(t):
                    wait_toks(t, [consts_w])
                    for nn in range(4):
                        t.wait_ge(sDV.h, dvq + nn + 1)
                        t.matmul(pj[nn], shA[:, :], u_sb[:, nn * 512:(nn + 1) * 512],
                                 start=True, stop=False)
                        winc(sPE, t.matmul(pj[nn], shB[:, :], v_sb[:, nn * 512:(nn + 1) * 512],
                                           start=False, stop=True))
                    for nn in range(4):
                        t.wait_ge(sDV.h, dvk + nn + 1)
                        t.matmul(pk[nn], shA[:, :], u2_sb[:, nn * 512:(nn + 1) * 512],
                                 start=True, stop=False)
                        winc(sPE, t.matmul(pk[nn], shB[:, :], v2_sb[:, nn * 512:(nn + 1) * 512],
                                           start=False, stop=True))
                blk.tensor(_)
                rpk = rpq + 4
                st['uv_free'] = (sPE, rpq + 8)
                st['uv2_free'] = (sPE, rpq + 8)

                pe0 = sPE.n
                smm_pred = {}
                av_pred = {}
                nn_ = pe0
                for c in range(4):
                    npr = (4 * c + 4) // 2
                    for p in range(npr):
                        nn_ += 1
                        smm_pred[(c, p)] = nn_
                    nn_ += 1
                    av_pred[c] = nn_
                pe_end = nn_

                sc0 = sSC.n
                dv0_pred = sDV.n
                etok_pred = {}
                sc_evac_pred = {}
                def _(sc, hp=hp):
                    wait_toks(sc, [st['qk_free']])
                    for nn in range(4):
                        sc.wait_ge(sPE.h, rpq + nn + 1)
                        winc(sSC, sc.copy(qhp[:, nn * 512:(nn + 1) * 512], pj[nn]))
                    for nn in range(4):
                        sc.wait_ge(sPE.h, rpk + nn + 1)
                        winc(sSC, sc.copy(khp[:, nn * 512:(nn + 1) * 512], pk[nn]))
                    for c in range(4):
                        ntl = 4 * c + 4
                        if c >= 1:
                            sc.wait_ge(sDV.h, dv0_pred + 6 * c - 2)
                        for p in range(ntl // 2):
                            t0p, t1p = 2 * p, 2 * p + 1
                            sc.wait_ge(sPE.h, smm_pred[(c, p)])
                            if t0p >= 4 * c:
                                for tt_ in (t0p, t1p):
                                    m = tt_ - 4 * c
                                    sc.activation(out=p0[:, tt_ * 512 + 128 * m: (tt_ + 1) * 512],
                                                  in_=psA[:, (tt_ % 2) * 512 + 128 * m: (tt_ % 2) * 512 + 512],
                                                  func=AF.Exp, scale=0.125)
                                    winc(sSC, sc.activation(
                                        out=p1[:, tt_ * 512 + 128 * m: (tt_ + 1) * 512],
                                        in_=psB[:, (tt_ % 2) * 512 + 128 * m: (tt_ % 2) * 512 + 512],
                                        func=AF.Exp, scale=0.125))
                                    etok_pred[(c, tt_)] = (sSC, sSC.n)
                            else:
                                sc.activation(out=p0[:, t0p * 512:(t0p + 2) * 512], in_=psA[:, :],
                                              func=AF.Exp, scale=0.125)
                                winc(sSC, sc.activation(out=p1[:, t0p * 512:(t0p + 2) * 512],
                                                        in_=psB[:, :], func=AF.Exp, scale=0.125))
                                etok_pred[(c, t0p)] = etok_pred[(c, t1p)] = (sSC, sSC.n)
                        sc.wait_ge(sPE.h, av_pred[c])
                        if c >= 1:
                            sc.wait_ge(sDV.h, dv0_pred + 6 * c)
                        winc(sSC, sc.activation(out=scr2[64:128, :], in_=psD[64:128, 512:1024],
                                                func=AF.Ln))
                        sc.wait_ge(sSC.h, sSC.n)
                        winc(sSC, sc.activation(out=scr2[64:128, :], in_=scr2[64:128, :],
                                                func=AF.Exp, scale=-1.0))
                        sc_evac_pred[c] = sSC.n
                blk.scalar(_)

                dv0 = sDV.n
                assert dv0 == dv0_pred + 0 or True
                mtok_pred = {}
                ev_pred = {}
                def _(v, hp=hp):
                    for c in range(4):
                        ntl = 4 * c + 4
                        for tt_ in range(4 * c, ntl):
                            m = tt_ - 4 * c
                            wait_toks(v, [etok_pred[(c, tt_)], consts_w])
                            ins = None
                            for pp2 in (p0, p1):
                                dsl = pp2[:, tt_ * 512 + 128 * m: tt_ * 512 + 128 * m + 128]
                                ins = v.tensor_mul(dsl, dsl, tri[:, :])
                                if m > 0:
                                    ins = v.memset(pp2[:, tt_ * 512: tt_ * 512 + 128 * m], 0.0)
                            winc(sDV, ins)
                            mtok_pred[(c, tt_)] = (sDV, sDV.n)
                        v.wait_ge(sPE.h, av_pred[c])
                        wait_toks(v, [st['act_free']])
                        v.wait_ge(sDV.h, sDV.n)
                        winc(sDV, v.reciprocal(scr2[0:64, :], psD[0:64, 0:512]))
                        v.wait_ge(sDV.h, sDV.n)
                        v.wait_ge(sSC.h, sc_evac_pred[c])
                        v.tensor_mul(att[0:64, hp * N + c * 512: hp * N + c * 512 + 512],
                                     psC[0:64, 0:512], scr2[0:64, :])
                        winc(sDV, v.tensor_mul(att[64:128, hp * N + c * 512: hp * N + c * 512 + 512],
                                               psC[64:128, 512:1024], scr2[64:128, :]))
                        ev_pred[c] = (sDV, sDV.n)
                blk.vector(_)
                assert sDV.n == dv0 + 24, (sDV.n, dv0)

                def _(t, hp=hp):
                    first = True
                    for c in range(4):
                        ntl = 4 * c + 4
                        npr = ntl // 2
                        for p in range(npr + 1):
                            if p < npr:
                                t0p, t1p = 2 * p, 2 * p + 1
                                w = []
                                if first:
                                    w = [v_done, consts_dv, (sSC, sc0 + 8)]
                                    first = False
                                if p >= 1:
                                    w.append(etok_pred[(c, 2 * p - 2)])
                                    w.append(etok_pred[(c, 2 * p - 1)])
                                    if 2 * p - 1 >= 4 * c:
                                        w.append(mtok_pred[(c, 2 * p - 2)])
                                        w.append(mtok_pred[(c, 2 * p - 1)])
                                elif c > 0:
                                    w.append(etok_pred[(c - 1, 4 * c - 1)])
                                    w.append(mtok_pred[(c - 1, 4 * c - 1)])
                                wait_toks(t, w)
                                score_mm(t, t0p, hp, c)
                                winc(sPE, score_mm(t, t1p, hp, c))
                            av_p = p - 1 if p <= npr - 1 else npr - 1
                            if av_p >= 0:
                                ta, tb = 2 * av_p, 2 * av_p + 1
                                w = [etok_pred[(c, ta)], etok_pred[(c, tb)]]
                                if ta >= 4 * c:
                                    w += [mtok_pred[(c, ta)], mtok_pred[(c, tb)]]
                                if av_p == 0:
                                    w += st['bank_pk'] + [v_done]
                                    if c > 0:
                                        w.append(ev_pred[c - 1])
                                wait_toks(t, w)
                                ins = emit_av_pair(t, ta, tb, hp, ntl)
                                if tb == ntl - 1:
                                    winc(sPE, ins)
                blk.tensor(_)
                assert sPE.n == pe_end, (sPE.n, pe_end)
                st['p_free'] = sPE.tok()
                st['bank_pj'] = [(sSC, sc_evac_pred[3])] * 4
                st['bank_pk'] = [ev_pred[3]]
                st['qk_free'] = sPE.tok()
            attn_tok = sDV.tok()

            for ot in range(8):
                px = pj if ot % 2 == 0 else [psC[:, 0:512], psC[:, 512:1024],
                                             psD[:, 0:512], psD[:, 512:1024]]
                def _(g, ot=ot):
                    slot_dma(g, sWba[ot % 2], [st['wba_free'][ot % 2]],
                             wba[ot % 2][:, :], wod[ot])
                blk.gpsimd(_)
                wtok = sWba[ot % 2].tok()

                def _(t, ot=ot, wtok=wtok, px=px):
                    w = [wtok, attn_tok]
                    w += st['bank_pj'] if ot % 2 == 0 else st['bank_pk']
                    wait_toks(t, w)
                    ins = None
                    for ic in range(8):
                        for nn in range(4):
                            ins = t.matmul(px[nn], wba[ot % 2][:, ic * 128:(ic + 1) * 128],
                                           att[:, ic * N + nn * 512: ic * N + nn * 512 + 512],
                                           start=(ic == 0), stop=(ic == 7))
                    winc(sPE, ins)
                blk.tensor(_)
                st['wba_free'][ot % 2] = sPE.tok()
                wo_mm = sPE.tok()

                for nn in range(4):
                    def _(sy, b=b, ot=ot, nn=nn):
                        slot_dma(sy, sLxr[nn % 2], [st['xre_free'][nn % 2]],
                                 xre[nn % 2][:, :],
                                 xT[b, ot * 128:(ot + 1) * 128, nn * 512:(nn + 1) * 512])
                    blk.sync(_)
                    xrl = sLxr[nn % 2].tok()

                    def _(v, nn=nn, wo_mm=wo_mm, xrl=xrl, px=px):
                        wait_toks(v, [wo_mm, xrl])
                        winc(sDV, v.tensor_add(xre[nn % 2][:, :], px[nn], xre[nn % 2][:, :]))
                    blk.vector(_)
                    x1v = sDV.tok()
                    if nn == 3:
                        if ot % 2 == 0:
                            st['bank_pj'] = [sDV.tok()] * 4
                        else:
                            st['bank_pk'] = [sDV.tok()]

                    def _(sy, b=b, ot=ot, nn=nn, x1v=x1v):
                        wait_toks(sy, [x1v, sSxr[nn % 2].tok()])
                        winc(sSxr[nn % 2], sy.dma_start(
                            out=x1s[b, ot * 128:(ot + 1) * 128, nn * 512:(nn + 1) * 512],
                            in_=xre[nn % 2][:, :]))
                        if mode == "attn_only":
                            winc(sSxr[nn % 2], sy.dma_start(
                                out=y[b, ot * 128:(ot + 1) * 128, nn * 512:(nn + 1) * 512],
                                in_=xre[nn % 2][:, :]))
                    blk.sync(_)
                    st['xre_free'][nn % 2] = sSxr[nn % 2].tok()

            st[('attn_done', b)] = [sSxr[0].tok(), sSxr[1].tok()]
            st['act_free'] = sPE.tok()
            st['vs_free'] = sPE.tok()
            st['p_free'] = sPE.tok()
        aes.close()

    def build_ffn():
        fes = contextlib.ExitStack()
        fsb = lambda n_, sh_, dt: fes.enter_context(nc.sbuf_tensor(n_, sh_, dt))
        xln2 = fsb("xln2", [128, 8 * 1024], BF16)
        hA = fsb("hA", [128, 32 * 1024], BF16)
        hB = fsb("hB", [128, 32 * 1024], BF16)
        silu = [fsb(f"silu{i}", [128, 1024], BF16) for i in range(2)]
        w1b = [fsb(f"w1b{i}", [128, 1024], BF16) for i in range(2)]
        w2b = [fsb(f"w2b{i}", [128, 1024], BF16) for i in range(2)]
        hwb = [fsb(f"hwb{i}", [128, 4096], BF16) for i in range(2)]
        x1l = [fsb(f"x1l{i}", [128, 512], F32) for i in range(2)]
        x1w = [fsb(f"x1w{i}", [128, 1024], F32) for i in range(2)]
        sq2 = [fsb(f"sq2{i}", [128, 512], BF16) for i in range(2)]
        xb2 = [fsb(f"xb2{i}", [128, 512], BF16) for i in range(4)]
        fscr = fsb("fscr", [128, 512], F32)

        x1src = xT if mode == "ffn_only" else x1s
        pairs = [psA, psB, psC, psD]

        if mode != "ffn_only":
            ALL_A = ([s.tok() for s in (sLt, sWv, sDV, sSC, sPE)]
                     + [s.tok() for s in sLxr + sWba + sSxr + sGx])
            pinit = list(ALL_A)
            aend_w = list(ALL_A)
        else:
            ALL_A = []
            aend_w = []
            pinit = []
        st.update(f_x1l_free=[ALL_A, ALL_A] if mode != "ffn_only" else [None, None],
                  f_xb_free=[ALL_A] * 4 if mode != "ffn_only" else [None] * 4,
                  f_stats_free=list(ALL_A),
                  f_w1b=[aend_w, aend_w], f_w2b=[aend_w, aend_w],
                  f_hwb=[aend_w, aend_w],
                  f_silu=[ALL_A, ALL_A] if mode != "ffn_only" else [None, None],
                  f_pair_free=[list(pinit) for _i in range(4)],
                  f_hA_free=[consts_ld] + ALL_A, f_hB_free=ALL_A,
                  f_x1w_free=[ALL_A, ALL_A] if mode != "ffn_only" else [None, None],
                  f_out_free=[[consts_ld] + ALL_A, [consts_ld] + ALL_A],
                  f_xln_free=ALL_A if mode != "ffn_only" else None)

        for nb in range(BPC * 2):
            b, nbo = nb // 2, nb % 2
            base_tok = list(ALL_A) if mode != "ffn_only" else []

            f_pend = [None]

            def ln2_flush():
                nsb, bcp, inv_tok, cons2, roff = f_pend[0]
                bc_tok = emit_ln_bc(bcp, inv_tok,
                                    st['f_pair_free'][nsb], roff)
                xldt2 = []
                dv_base = sDV.n
                def _(sy, nsb=nsb):
                    for ic in range(8):
                        g_ = list(cons2[6 + (ic % 2)]) + _aslist(st['f_x1l_free'][ic % 2])
                        if ic >= 2:
                            g_.append((sDV, dv_base + 2 * ic - 2))
                        slot_dma(sy, sLx1l[ic % 2], g_,
                                 x1l[ic % 2][:, :],
                                 x1src[b, ic * 128:(ic + 1) * 128,
                                       nbo * 1024 + nsb * 512: nbo * 1024 + nsb * 512 + 512])
                        xldt2.append(sLx1l[ic % 2].tok())
                blk.sync(_)

                norm_dv = sDV.n
                st.setdefault('f_norm_dv', {})[nsb] = norm_dv
                def _(v, nsb=nsb, bcp=bcp):
                    wait_toks(v, [bc_tok] + _aslist(st['f_xln_free']))
                    for ic in range(8):
                        wait_toks(v, [xldt2[ic]])
                        v.wait_ge(sDV.h, sDV.n)
                        winc(sDV, v.tensor_sub(fscr[:, :], x1l[ic % 2][:, :], bcp[:, 0:512]))
                        v.wait_ge(sDV.h, sDV.n)
                        winc(sDV, v.tensor_mul(
                            xln2[:, ic * 1024 + nsb * 512: ic * 1024 + nsb * 512 + 512],
                            fscr[:, :], bcp[:, 512:1024]))
                blk.vector(_)
                st['f_x1l_free'] = [sDV.tok(), sDV.tok()]
                st['f_pair_free'][nsb] = [sDV.tok()]
                f_pend[0] = None

            for nsb in range(2):
                stp = psC if nsb == 0 else psD
                bcp = psA if nsb == 0 else psB
                xldt = []
                pe_base0 = sPE.n
                dv_base0 = sDV.n
                def _(g, nsb=nsb):
                    for ic in range(8):
                        g_ = _aslist(st['f_xb_free'][ic % 4]) + base_tok
                        if ic >= 4:
                            g_ += [(sPE, pe_base0 + ic - 3), (sDV, dv_base0 + ic - 3)]
                        slot_dma(g, sGx[ic % 4], g_,
                                 xb2[ic % 4][:, :],
                                 x1src[b, ic * 128:(ic + 1) * 128,
                                       nbo * 1024 + nsb * 512: nbo * 1024 + nsb * 512 + 512])
                        xldt.append(sGx[ic % 4].tok())
                blk.gpsimd(_)

                inv_tok, rows_tok, cons2 = emit_ln_stats(
                    lambda ic: xb2[ic % 4][:, :], sq2, stp,
                    st['f_stats_free'] + st['f_pair_free'][2 + nsb], [],
                    xldt, nsb * 1024)
                st['f_stats_free'] = [rows_tok]
                st['f_pair_free'][2 + nsb] = [rows_tok]
                st['f_xb_free'] = [cons2[4], cons2[5], cons2[6], cons2[7]]

                if f_pend[0] is not None:
                    ln2_flush()
                f_pend[0] = (nsb, bcp, inv_tok, cons2, nsb * 1024)
            ln2_flush()
            xln2_tok = sDV.tok()

            g1_base = [sW1[0].n, sW1[1].n]
            g2_base = [sW2[0].n, sW2[1].n]
            pe_b = sPE.n
            sc_b = sSC.n
            dv_b = sDV.n
            def _(g):
                for ot in range(32):
                    g_ = _aslist(st['f_w1b'][ot % 2]) if ot < 2 else [(sPE, pe_b + 2 * ot - 3)]
                    slot_dma(g, sW1[ot % 2], g_, w1b[ot % 2][:, :], w1d[ot])
                    g_ = _aslist(st['f_w2b'][ot % 2]) if ot < 2 else [(sPE, pe_b + 2 * ot - 2)]
                    slot_dma(g, sW2[ot % 2], g_, w2b[ot % 2][:, :], w2d[ot])
            blk.gpsimd(_)

            def _(t):
                for ot in range(32):
                    pw1 = pairs[(ot % 2) * 2]
                    pw2 = pairs[(ot % 2) * 2 + 1]
                    w = [(sW1[ot % 2], g1_base[ot % 2] + 16 * (ot // 2 + 1)),
                         (sW2[ot % 2], g2_base[ot % 2] + 16 * (ot // 2 + 1))]
                    if ot == 1:
                        w.append(xln2_tok)
                    if ot < 2:
                        w += st['f_pair_free'][(ot % 2) * 2] + st['f_pair_free'][(ot % 2) * 2 + 1]
                    else:
                        w += [(sSC, sc_b + ot - 1), (sDV, dv_b + ot - 1)]
                    wait_toks(t, w)
                    for wb_, ps_ in ((w1b, pw1), (w2b, pw2)):
                        ins = None
                        for ic in range(8):
                            for nsb in range(2):
                                if ot == 0 and wb_ is w1b:
                                    wait_toks(t, [(sDV, st['f_norm_dv'][nsb] + 2 * ic + 2)])
                                ins = t.matmul(ps_[:, nsb * 512:(nsb + 1) * 512],
                                               wb_[ot % 2][:, ic * 128:(ic + 1) * 128],
                                               xln2[:, ic * 1024 + nsb * 512: ic * 1024 + nsb * 512 + 512],
                                               start=(ic == 0), stop=(ic == 7))
                        winc(sPE, ins)
            blk.tensor(_)

            def _(sc):
                for ot in range(32):
                    pw1 = pairs[(ot % 2) * 2]
                    w = [(sPE, pe_b + 2 * ot + 1)]
                    if ot < 2:
                        w += _aslist(st['f_silu'][ot % 2])
                    else:
                        w.append((sDV, dv_b + ot - 1))
                    wait_toks(sc, w)
                    winc(sSC, sc.activation(out=silu[ot % 2][:, :], in_=pw1[:, :], func=AF.Silu,
                                            scale=s1v[:, ot:ot + 1]))
            blk.scalar(_)

            def _(v):
                for ot in range(32):
                    pw2 = pairs[(ot % 2) * 2 + 1]
                    w = [(sPE, pe_b + 2 * ot + 2), (sSC, sc_b + ot + 1)]
                    if ot == 0:
                        w += _aslist(st['f_hA_free'])
                    wait_toks(v, w)
                    winc(sDV, v.scalar_tensor_tensor(
                        hA[:, ot * 1024:(ot + 1) * 1024], pw2[:, :], s2v[:, ot:ot + 1],
                        silu[ot % 2][:, :], ALU.mult, ALU.mult))
            blk.vector(_)
            st['f_w1b'] = [(sPE, pe_b + 61), (sPE, pe_b + 63)]
            st['f_w2b'] = [(sPE, pe_b + 62), (sPE, pe_b + 64)]
            st['f_silu'] = [(sDV, dv_b + 31), (sDV, dv_b + 32)]
            st['f_pair_free'] = [[(sSC, sc_b + 31)], [(sDV, dv_b + 31)],
                                 [(sSC, sc_b + 32)], [(sDV, dv_b + 32)]]
            hA_tok = sDV.tok()

            for li, (wsrc, scl, src_, dst_) in enumerate([
                    (h1d, sh1v, hA, hB), (h2d, sh2v, hB, hA)]):
                gh_base = [sWh[0].n, sWh[1].n]
                pe_b = sPE.n
                sc_b = sSC.n
                def _(g, wsrc=wsrc, pe_b=pe_b):
                    for ot in range(32):
                        g_ = (_aslist(st['f_hwb'][ot % 2]) if ot < 2
                              else [(sPE, pe_b + ot - 1)])
                        slot_dma(g, sWh[ot % 2], g_, hwb[ot % 2][:, :], wsrc[ot])
                blk.gpsimd(_)

                def _(t, li=li, src_=src_, pe_b=pe_b, sc_b=sc_b, gh_base=gh_base):
                    for ot in range(32):
                        ps_ = pairs[ot % 2]
                        w = [(sWh[ot % 2], gh_base[ot % 2] + 16 * (ot // 2 + 1))]
                        if ot < 2:
                            w += st['f_pair_free'][ot % 2]
                            w.append(hA_tok if li == 0 else l1_all)
                        else:
                            w.append((sSC, sc_b + ot - 1))
                        wait_toks(t, w)
                        ins = None
                        for ic in range(32):
                            for nsb in range(2):
                                ins = t.matmul(ps_[:, nsb * 512:(nsb + 1) * 512],
                                               hwb[ot % 2][:, ic * 128:(ic + 1) * 128],
                                               src_[:, ic * 1024 + nsb * 512: ic * 1024 + nsb * 512 + 512],
                                               start=(ic == 0), stop=(ic == 31))
                        winc(sPE, ins)
                blk.tensor(_)

                def _(sc, li=li, dst_=dst_, scl=scl, pe_b=pe_b):
                    for ot in range(32):
                        ps_ = pairs[ot % 2]
                        w = [(sPE, pe_b + ot + 1)]
                        if ot == 0:
                            w += _aslist(st['f_hB_free'] if li == 0 else st['f_hA2_free'])
                        wait_toks(sc, w)
                        winc(sSC, sc.activation(out=dst_[:, ot * 1024:(ot + 1) * 1024],
                                                in_=ps_[:, :], func=AF.Gelu,
                                                scale=scl[:, ot:ot + 1]))
                blk.scalar(_)
                st['f_hwb'] = [(sPE, pe_b + 31), (sPE, pe_b + 32)]
                st['f_pair_free'][0] = [(sSC, sc_b + 31)]
                st['f_pair_free'][1] = [(sSC, sc_b + 32)]
                if li == 0:
                    l1_all = sSC.tok()
                    st['f_hA2_free'] = None
            hh_tok = sSC.tok()
            st['f_hB_free'] = None

            gh_base = [sWh[0].n, sWh[1].n]
            pe_b = sPE.n
            dv_b = sDV.n
            def _(g):
                for ot in range(8):
                    g_ = (_aslist(st['f_hwb'][ot % 2]) if ot < 2
                          else [(sPE, pe_b + ot - 1)])
                    slot_dma(g, sWh[ot % 2], g_, hwb[ot % 2][:, :], w3d[ot])
            blk.gpsimd(_)

            x1w_base = [sLx1w[0].n, sLx1w[1].n]
            so_base = [sSo[0].n, sSo[1].n]
            def _(sy):
                for ot in range(10):
                    if ot >= 2:
                        k = ot - 2
                        wait_toks(sy, [(sDV, dv_b + k + 1), sSo[k % 2].tok()])
                        winc(sSo[k % 2], sy.dma_start(
                            out=y[b, k * 128:(k + 1) * 128, nbo * 1024:(nbo + 1) * 1024],
                            in_=x1w[k % 2][:, :]))
                    if ot < 8:
                        g_ = list(base_tok)
                        if ot < 2:
                            g_ += _aslist(st['f_x1w_free'][ot % 2])
                        else:
                            g_ += [(sDV, dv_b + ot - 1),
                                   (sSo[ot % 2], so_base[ot % 2] + 16 * (ot // 2))]
                        slot_dma(sy, sLx1w[ot % 2], g_,
                                 x1w[ot % 2][:, :],
                                 x1src[b, ot * 128:(ot + 1) * 128, nbo * 1024:(nbo + 1) * 1024])
            blk.sync(_)

            def _(t, pe_b=pe_b, dv_b=dv_b, gh_base=gh_base):
                for ot in range(8):
                    ps_ = pairs[ot % 2]
                    w = [(sWh[ot % 2], gh_base[ot % 2] + 16 * (ot // 2 + 1)), hh_tok]
                    if ot < 2:
                        w += st['f_pair_free'][ot % 2]
                    else:
                        w.append((sDV, dv_b + ot - 1))
                    wait_toks(t, w)
                    ins = None
                    for ic in range(32):
                        for nsb in range(2):
                            ins = t.matmul(ps_[:, nsb * 512:(nsb + 1) * 512],
                                           hwb[ot % 2][:, ic * 128:(ic + 1) * 128],
                                           hA[:, ic * 1024 + nsb * 512: ic * 1024 + nsb * 512 + 512],
                                           start=(ic == 0), stop=(ic == 31))
                    winc(sPE, ins)
            blk.tensor(_)

            def _(v, pe_b=pe_b):
                for ot in range(8):
                    ps_ = pairs[ot % 2]
                    w = [(sPE, pe_b + ot + 1),
                         (sLx1w[ot % 2], x1w_base[ot % 2] + 16 * (ot // 2 + 1))]
                    if ot < 2:
                        w += _aslist(st['f_out_free'][ot % 2])
                    else:
                        w.append((sSo[ot % 2], so_base[ot % 2] + 16 * (ot // 2)))
                    wait_toks(v, w)
                    winc(sDV, v.scalar_tensor_tensor(
                        x1w[ot % 2][:, :], ps_[:, :], s3v[:, ot:ot + 1],
                        x1w[ot % 2][:, :], ALU.mult, ALU.add))
            blk.vector(_)

            st['f_hwb'] = [(sPE, pe_b + 7), (sPE, pe_b + 8)]
            st['f_x1w_free'] = [[(sDV, dv_b + 7), (sSo[0], so_base[0] + 64)],
                                [(sDV, dv_b + 8), (sSo[1], so_base[1] + 64)]]
            st['f_pair_free'][0] = [(sDV, dv_b + 7)]
            st['f_pair_free'][1] = [(sDV, dv_b + 8)]
            st['f_out_free'] = [sSo[0].tok(), sSo[1].tok()]

            nb_end = sPE.tok()
            st['f_hA_free'] = nb_end
            st['f_hB_free'] = nb_end
            st['f_hA2_free'] = None
            st['f_xln_free'] = nb_end
        fes.close()

    if mode != "ffn_only":
        build_attention()
    if mode != "attn_only":
        build_ffn()

    def _(sy):
        wait_toks(sy, [sSxr[0].tok(), sSxr[1].tok(), sSo[0].tok(), sSo[1].tok()])
    blk.sync(_)

    es.close()
    return nc


def _rope_perm():
    perm = []
    for t in range(8):
        for hh in (2 * t, 2 * t + 1):
            perm += [hh * 64 + 2 * j for j in range(32)]
            perm += [hh * 64 + 2 * j + 1 for j in range(32)]
    return np.array(perm)


def _block_w(wt, not_):
    d_in, d_out = wt.shape
    nic = d_in // 128
    a = wt.reshape(nic, 128, not_, 128)
    return np.ascontiguousarray(a.transpose(2, 1, 0, 3).reshape(not_, 128, nic * 128))


def _tables():
    j = np.arange(32, dtype=np.float32)
    exponent = (j * np.float32(2.0 / 64.0)).astype(np.float32)
    inv_freq = (np.float32(10000.0) ** (-exponent)).astype(np.float32)
    pos = np.arange(N, dtype=np.float32)
    ang = (pos[:, None] * inv_freq[None, :]).astype(np.float32)
    cos = np.cos(ang).T.astype(np.float32)
    sin = np.sin(ang).T.astype(np.float32)
    t1 = np.concatenate([cos, sin, cos, sin], 0)
    t2 = np.concatenate([sin, cos, sin, cos], 0)
    A = np.zeros((128, 128), np.float32)
    Bm = np.zeros((128, 128), np.float32)
    for blk0 in (0, 64):
        for jj in range(32):
            m1, m2 = blk0 + jj, blk0 + 32 + jj
            A[m1, m1] = 1.0
            A[m2, m1] = -1.0
            Bm[m1, m2] = 1.0
            Bm[m2, m2] = 1.0
    tri_ = np.zeros((128, 128), np.float32)
    for kk in range(128):
        tri_[kk, kk:] = 1.0
    return t1, t2, np.ascontiguousarray(np.concatenate([A, Bm], 1)), tri_


_prog_cache = {}


def kernel(L0_attn_x, L0_attn_Wq, L0_attn_Wk, L0_attn_Wv, L0_attn_Wo,
           L0_ffn_w1_w, L0_ffn_w1_s, L0_ffn_w2_w, L0_ffn_w2_s,
           L0_ffn_h1_w, L0_ffn_h1_s, L0_ffn_h2_w, L0_ffn_h2_s,
           L0_ffn_w3_w, L0_ffn_w3_s):
    mode = _CFG["mode"]
    x = np.ascontiguousarray(np.asarray(L0_attn_x, dtype=np.float32))
    perm = _rope_perm()
    wq_t = np.asarray(L0_attn_Wq, np.float32).T[:, perm]
    wk_t = np.asarray(L0_attn_Wk, np.float32).T[:, perm]
    wv_t = np.asarray(L0_attn_Wv, np.float32).T
    wo_t = np.asarray(L0_attn_Wo, np.float32).T
    w1_t = np.asarray(L0_ffn_w1_w, np.float32).T
    w2_t = np.asarray(L0_ffn_w2_w, np.float32).T
    h1_t = np.asarray(L0_ffn_h1_w, np.float32).T
    h2_t = np.asarray(L0_ffn_h2_w, np.float32).T
    w3_t = np.asarray(L0_ffn_w3_w, np.float32).T

    wv_r = np.ascontiguousarray(
        wv_t.reshape(8, 128, 2, 512).transpose(2, 1, 0, 3).reshape(2, 128, 4096))
    t1, t2, sh, tri_ = _tables()
    shared = {
        "wq": _block_w(wq_t, 8), "wk": _block_w(wk_t, 8), "wv": wv_r,
        "wo": _block_w(wo_t, 8),
        "w1": _block_w(w1_t, 32), "w2": _block_w(w2_t, 32),
        "h1w": _block_w(h1_t, 32), "h2w": _block_w(h2_t, 32),
        "w3": _block_w(w3_t, 8),
        "s1": np.ascontiguousarray(np.asarray(L0_ffn_w1_s, np.float32).reshape(32, 128).T),
        "s2": np.ascontiguousarray(np.asarray(L0_ffn_w2_s, np.float32).reshape(32, 128).T),
        "sh1": np.ascontiguousarray(np.asarray(L0_ffn_h1_s, np.float32).reshape(32, 128).T),
        "sh2": np.ascontiguousarray(np.asarray(L0_ffn_h2_s, np.float32).reshape(32, 128).T),
        "s3": np.ascontiguousarray(np.asarray(L0_ffn_w3_s, np.float32).reshape(8, 128).T),
        "t1": t1, "t2": t2, "sh": sh, "tri": tri_,
    }

    if mode not in _prog_cache:
        _prog_cache[mode] = build_program(mode)
    nc = _prog_cache[mode]

    in_maps = []
    for c in range(NCORE):
        xc = np.ascontiguousarray(x[BPC * c: BPC * (c + 1)].transpose(0, 2, 1))
        in_maps.append({"xT": xc, **shared})

    res = run_bass_kernel_spmd(nc, in_maps, core_ids=list(range(NCORE)))
    out = np.empty((B, N, D), np.float32)
    for c in range(NCORE):
        out[BPC * c: BPC * (c + 1)] = np.asarray(res.results[c]["y"]).transpose(0, 2, 1)
    return out


# version 12 (final)
# speedup vs baseline: 1.0926x; 1.0926x over previous
import contextlib
import numpy as np
import ml_dtypes

import concourse.bass as bass
import concourse.mybir as mybir
from concourse.bass_utils import run_bass_kernel_spmd

F32 = mybir.dt.float32
BF16 = mybir.dt.bfloat16
AF = mybir.ActivationFunctionType
ALU = mybir.AluOpType

B, N, D, H, DK = 16, 2048, 1024, 16, 64
DH = 4096
BPC = 2
NCORE = 8
EPS = 1e-5

_CFG = {"mode": "full"}


class Sem:
    def __init__(self, nc, es, name, step=1):
        self.h = es.enter_context(nc.semaphore(name))
        self.n = 0
        self.step = step

    def tok(self):
        return (self, self.n)


def winc(sem, ins):
    ins.then_inc(sem.h, sem.step)
    sem.n += sem.step


def _aslist(t):
    return t if isinstance(t, list) else [t]


def slot_dma(eng, slot_sem, gates, out, in_):
    wait_toks(eng, gates + [slot_sem.tok()])
    winc(slot_sem, eng.dma_start(out=out, in_=in_))
    return slot_sem.tok()


def wait_toks(eng, toks):
    best = {}
    for t in toks:
        if t is None:
            continue
        s, v = t
        if v <= 0:
            continue
        if best.get(id(s), (None, -1))[1] < v:
            best[id(s)] = (s, v)
    for s, v in best.values():
        eng.wait_ge(s.h, v)


def build_program(mode="full"):
    nc = bass.Bass("TRN2", target_bir_lowering=False)

    dt_in = lambda n_, sh_: nc.dram_tensor(n_, sh_, F32, kind="ExternalInput").ap()
    xT = dt_in("xT", [BPC, D, N])
    wqd = dt_in("wq", [8, 128, D])
    wkd = dt_in("wk", [8, 128, D])
    wvd = dt_in("wv", [2, 128, 8 * 512])
    wod = dt_in("wo", [8, 128, D])
    w1d = dt_in("w1", [32, 128, D])
    w2d = dt_in("w2", [32, 128, D])
    h1d = dt_in("h1w", [32, 128, DH])
    h2d = dt_in("h2w", [32, 128, DH])
    w3d = dt_in("w3", [8, 128, DH])
    s1d = dt_in("s1", [128, 32])
    s2d = dt_in("s2", [128, 32])
    sh1d = dt_in("sh1", [128, 32])
    sh2d = dt_in("sh2", [128, 32])
    s3d = dt_in("s3", [128, 8])
    t1d = dt_in("t1", [128, N])
    t2d = dt_in("t2", [128, N])
    shd = dt_in("sh", [128, 256])
    trid = dt_in("tri", [128, 128])
    y = nc.dram_tensor("y", [BPC, D, N], F32, kind="ExternalOutput").ap()
    x1s = nc.dram_tensor("x1s", [BPC, D, N], F32, kind="Internal").ap()

    es = contextlib.ExitStack()
    sb = lambda n_, sh_, dt: es.enter_context(nc.sbuf_tensor(n_, sh_, dt))

    shA = sb("shA", [128, 128], BF16)
    shB = sb("shB", [128, 128], BF16)
    tri = sb("tri_c", [128, 128], BF16)
    ones_bf = sb("ones_bf", [128, 128], BF16)
    ones_f = sb("ones_f", [1, 128], F32)
    srow = sb("srow", [1, 5 * 512], F32)
    s1v = sb("s1v", [128, 32], F32)
    s2v = sb("s2v", [128, 32], F32)
    sh1v = sb("sh1v", [128, 32], F32)
    sh2v = sb("sh2v", [128, 32], F32)
    s3v = sb("s3v", [128, 8], F32)

    psA = es.enter_context(nc.psum_tensor("psA", [128, 1024], F32))
    psB = es.enter_context(nc.psum_tensor("psB", [128, 1024], F32))
    psC = es.enter_context(nc.psum_tensor("psC", [128, 1024], F32))
    psD = es.enter_context(nc.psum_tensor("psD", [128, 1024], F32))

    sLc = Sem(nc, es, "sLc", 16)
    sLt = Sem(nc, es, "sLt", 16)
    sLxt = [Sem(nc, es, f"sLxt{i}", 16) for i in range(4)]
    sLxr = [Sem(nc, es, f"sLxr{i}", 16) for i in range(2)]
    sLx1l = [Sem(nc, es, f"sLx1l{i}", 16) for i in range(2)]
    sLx1w = [Sem(nc, es, f"sLx1w{i}", 16) for i in range(2)]
    sSxr = [Sem(nc, es, f"sSxr{i}", 16) for i in range(2)]
    sSo = [Sem(nc, es, f"sSo{i}", 16) for i in range(2)]
    sCW = Sem(nc, es, "sCW", 16)
    sGx = [Sem(nc, es, f"sGx{i}", 16) for i in range(4)]
    sWba = [Sem(nc, es, f"sWba{i}", 16) for i in range(2)]
    sWv = Sem(nc, es, "sWv", 16)
    sW1 = [Sem(nc, es, f"sW1{i}", 16) for i in range(2)]
    sW2 = [Sem(nc, es, f"sW2{i}", 16) for i in range(2)]
    sWh = [Sem(nc, es, f"sWh{i}", 16) for i in range(2)]
    sPE = Sem(nc, es, "sPE")
    sSC = Sem(nc, es, "sSC")
    sDV = Sem(nc, es, "sDV")

    blk = es.enter_context(nc.Block())

    def _(sy):
        for dst, src in [(s1v, s1d[:, :]), (s2v, s2d[:, :]),
                         (sh1v, sh1d[:, :]), (sh2v, sh2d[:, :]), (s3v, s3d[:, :])]:
            winc(sLc, sy.dma_start(out=dst[:, :], in_=src))
    blk.sync(_)
    consts_ld = sLc.tok()

    def _(g):
        for dst, src in [(shA, shd[:, 0:128]), (shB, shd[:, 128:256]),
                         (tri, trid[:, :])]:
            winc(sCW, g.dma_start(out=dst[:, :], in_=src))
    blk.gpsimd(_)
    consts_w = sCW.tok()

    def _(v):
        v.memset(ones_bf[:, :], 1.0)
        winc(sDV, v.memset(ones_f[:, :], 1.0))
    blk.vector(_)
    consts_dv = sDV.tok()

    st = {}

    def emit_ln_stats(src_bf, sqb, st_pair, extra_pe_toks, extra_dv_toks,
                      ld_toks, roff):
        pe_base = sPE.n
        dv_toks = []
        def _(v):
            wait_toks(v, extra_dv_toks)
            for ic in range(8):
                w = [ld_toks[ic]]
                if ic >= 2:
                    w.append((sPE, pe_base + ic - 1))
                wait_toks(v, w)
                winc(sDV, v.tensor_mul(sqb[ic % 2][:, :], src_bf(ic), src_bf(ic)))
                dv_toks.append(sDV.tok())
        blk.vector(_)

        def _(t):
            wait_toks(t, [consts_dv] + extra_pe_toks)
            for ic in range(8):
                wait_toks(t, [dv_toks[ic]])
                t.matmul(st_pair[0:1, 0:512], ones_bf[:, 0:1], src_bf(ic),
                         start=(ic == 0), stop=(ic == 7))
                winc(sPE, t.matmul(st_pair[0:1, 512:1024], ones_bf[:, 0:1], sqb[ic % 2][:, :],
                                   start=(ic == 0), stop=(ic == 7)))
        blk.tensor(_)
        assert sPE.n == pe_base + 8
        stats_mm = sPE.tok()
        cons_toks = [[(sPE, pe_base + ic + 1), dv_toks[ic]] for ic in range(8)]

        def _(v):
            wait_toks(v, [stats_mm])
            v.tensor_scalar_mul(srow[:, roff:roff + 512], st_pair[0:1, 0:512], 1.0 / D)
            winc(sDV, v.tensor_scalar_mul(srow[:, roff + 512:roff + 1024],
                                          st_pair[0:1, 512:1024], 1.0 / D))
            v.wait_ge(sDV.h, sDV.n)
            winc(sDV, v.tensor_mul(srow[:, 2048:2560],
                                   srow[:, roff:roff + 512], srow[:, roff:roff + 512]))
            v.wait_ge(sDV.h, sDV.n)
            winc(sDV, v.tensor_sub(srow[:, roff + 512:roff + 1024],
                                   srow[:, roff + 512:roff + 1024],
                                   srow[:, 2048:2560]))
            v.wait_ge(sDV.h, sDV.n)
            winc(sDV, v.tensor_scalar_add(srow[:, roff + 512:roff + 1024],
                                          srow[:, roff + 512:roff + 1024], EPS))
        blk.vector(_)
        rows_tok = sDV.tok()

        def _(sc):
            wait_toks(sc, [rows_tok])
            winc(sSC, sc.activation(out=srow[:, roff + 512:roff + 1024],
                                    in_=srow[:, roff + 512:roff + 1024], func=AF.Ln))
            sc.wait_ge(sSC.h, sSC.n)
            winc(sSC, sc.activation(out=srow[:, roff + 512:roff + 1024],
                                    in_=srow[:, roff + 512:roff + 1024],
                                    func=AF.Exp, scale=-0.5))
        blk.scalar(_)
        inv_tok = sSC.tok()
        return inv_tok, rows_tok, cons_toks

    def emit_ln_bc(bc_pair, inv_tok, bc_free, roff):
        def _(t):
            wait_toks(t, [inv_tok] + bc_free)
            t.matmul(bc_pair[:, 0:512], ones_f[0:1, :], srow[0:1, roff:roff + 512],
                     start=True, stop=True)
            winc(sPE, t.matmul(bc_pair[:, 512:1024], ones_f[0:1, :],
                               srow[0:1, roff + 512:roff + 1024], start=True, stop=True))
        blk.tensor(_)
        return sPE.tok()

    def build_attention():
        aes = contextlib.ExitStack()
        asb = lambda n_, sh_, dt: aes.enter_context(nc.sbuf_tensor(n_, sh_, dt))
        t1 = asb("t1s", [128, N], F32)
        t2 = asb("t2s", [128, N], F32)
        act = asb("act", [128, 8 * N], BF16)
        att = asb("att", [128, 8 * N], BF16)
        qhp = asb("qhp", [128, N], BF16)
        khp = asb("khp", [128, N], BF16)
        vs = asb("vs", [128, 16 * 1024], BF16)
        p0 = asb("p0", [128, 16 * 512], BF16)
        p1 = asb("p1", [128, 16 * 512], BF16)
        xb_t = [asb(f"xbt{i}", [128, 512], BF16) for i in range(4)]
        wvs = asb("wvs", [128, 8 * 512], BF16)
        wba = [asb(f"wba{i}", [128, 8 * 128], BF16) for i in range(2)]
        sq_t = [asb(f"sqt{i}", [128, 512], BF16) for i in range(2)]
        u_sb = asb("u_sb", [128, 4 * 512], BF16)
        v_sb = asb("v_sb", [128, 4 * 512], BF16)
        u2_sb = asb("u2_sb", [128, 4 * 512], BF16)
        v2_sb = asb("v2_sb", [128, 4 * 512], BF16)
        scr = asb("scr", [128, 512], F32)
        scr2 = scr
        xre = [asb(f"xre{i}", [128, 512], F32) for i in range(2)]

        def _(sy):
            winc(sLt, sy.dma_start(out=t1[:, :], in_=t1d[:, :]))
            winc(sLt, sy.dma_start(out=t2[:, :], in_=t2d[:, :]))
        blk.sync(_)
        t12_tok = sLt.tok()

        pj = [psA[:, 0:512], psA[:, 512:1024], psB[:, 0:512], psB[:, 512:1024]]
        rp = [psC[:, 0:512], psC[:, 512:1024], psD[:, 0:512], psD[:, 512:1024]]

        st.update(bank_pj=[None] * 4, bank_pk=[None, None], wba_free=[None, None],
                  xt_free=[None, None, None, None], stats_free=[], uv_free=None, uv2_free=None,
                  act_free=None, qk_free=None, vs_free=None, p_free=None,
                  xre_free=[None, None])

        for b in range(BPC):
            ln_state = {'stats_free_e': st['stats_free'] + st['bank_pk'],
                        'stats_free_o': st['stats_free'] + st['bank_pk'],
                        'bc_free_e': st['bank_pj'], 'bc_free_o': st['bank_pj']}
            pend = [None]

            def ln1_flush(b):
                nch, ev, bcp, inv_tok, cons1, roff = pend[0]
                bc_tok = emit_ln_bc(bcp, inv_tok,
                                    ln_state['bc_free_e' if ev else 'bc_free_o'], roff)
                xldt2 = []
                dv_base = sDV.n
                def _(sy, nch=nch):
                    for ic in range(8):
                        g_ = list(cons1[6 + (ic % 2)]) + _aslist(st['xre_free'][ic % 2])
                        if ic >= 2:
                            g_.append((sDV, dv_base + 2 * ic - 2))
                        slot_dma(sy, sLxr[ic % 2], g_,
                                 xre[ic % 2][:, :],
                                 xT[b, ic * 128:(ic + 1) * 128, nch * 512:(nch + 1) * 512])
                        xldt2.append(sLxr[ic % 2].tok())
                blk.sync(_)

                norm_dv = sDV.n
                st.setdefault('ln_norm_dv', {})[nch] = norm_dv
                def _(v, nch=nch, bcp=bcp):
                    wait_toks(v, [bc_tok, st['act_free']])
                    for ic in range(8):
                        wait_toks(v, [xldt2[ic]])
                        v.wait_ge(sDV.h, sDV.n)
                        winc(sDV, v.tensor_sub(scr[:, :], xre[ic % 2][:, :], bcp[:, 0:512]))
                        v.wait_ge(sDV.h, sDV.n)
                        winc(sDV, v.tensor_mul(
                            act[:, ic * N + nch * 512: ic * N + nch * 512 + 512],
                            scr[:, :], bcp[:, 512:1024]))
                blk.vector(_)
                st['xre_free'] = [sDV.tok(), sDV.tok()]
                if ev:
                    ln_state['bc_free_e'] = [sDV.tok()]
                else:
                    ln_state['bc_free_o'] = [sDV.tok()]
                pend[0] = None

            for nch in range(4):
                ev = nch % 2 == 0
                stp = psD if ev else psC
                bcp = psA if ev else psB
                xldt = []
                pe_base0 = sPE.n
                dv_base0 = sDV.n
                def _(g, b=b, nch=nch):
                    for ic in range(8):
                        g_ = _aslist(st['xt_free'][ic % 4]) + [consts_ld]
                        if ic >= 4:
                            g_ += [(sPE, pe_base0 + ic - 3), (sDV, dv_base0 + ic - 3)]
                        slot_dma(g, sGx[ic % 4], g_,
                                 xb_t[ic % 4][:, :],
                                 xT[b, ic * 128:(ic + 1) * 128, nch * 512:(nch + 1) * 512])
                        xldt.append(sGx[ic % 4].tok())
                blk.gpsimd(_)

                inv_tok, rows_tok, cons1 = emit_ln_stats(
                    lambda ic: xb_t[ic % 4][:, :], sq_t, stp,
                    ln_state['stats_free_e' if ev else 'stats_free_o'], [],
                    xldt, (nch % 2) * 1024)
                if ev:
                    ln_state['stats_free_e'] = [rows_tok]
                else:
                    ln_state['stats_free_o'] = [rows_tok]
                st['xt_free'] = [cons1[4], cons1[5], cons1[6], cons1[7]]

                if pend[0] is not None:
                    ln1_flush(b)
                pend[0] = (nch, ev, bcp, inv_tok, cons1, (nch % 2) * 1024)
            ln1_flush(b)
            xln_tok = sDV.tok()
            st['stats_free'] = ln_state['stats_free_e']
            st['bank_pk'] = [xln_tok]
            st['bank_pj'] = [xln_tok] * 4

            vevac = []
            vsc0 = sSC.n
            for dvc in range(2):
                def _(g, dvc=dvc):
                    g_ = [(sSC, vsc0 + 16)] if dvc == 1 else []
                    slot_dma(g, sWv, g_, wvs[:, :], wvd[dvc])
                blk.gpsimd(_)
                wv_tok = sWv.tok()

                vpe0 = sPE.n
                def _(t, dvc=dvc, wv_tok=wv_tok):
                    for nt in range(16):
                        gi = dvc * 16 + nt
                        toks = [wv_tok, xln_tok]
                        if gi < 4:
                            toks += st['bank_pj']
                        else:
                            toks.append((sSC, vsc0 + gi - 3))
                        wait_toks(t, toks)
                        ins = None
                        for ic in range(8):
                            ins = t.matmul(pj[gi % 4],
                                           act[:, ic * N + nt * 128: ic * N + nt * 128 + 128],
                                           wvs[:, ic * 512:(ic + 1) * 512],
                                           start=(ic == 0), stop=(ic == 7))
                        winc(sPE, ins)
                blk.tensor(_)

                def _(sc, dvc=dvc, vpe0=vpe0):
                    for nt in range(16):
                        gi = dvc * 16 + nt
                        wait_toks(sc, [(sPE, vpe0 + nt + 1), st['vs_free']])
                        winc(sSC, sc.copy(vs[:, nt * 1024 + dvc * 512: nt * 1024 + dvc * 512 + 512],
                                          pj[gi % 4]))
                blk.scalar(_)
            vevac = [(sSC, vsc0 + k + 1) for k in range(32)]
            st['bank_pj'] = [sSC.tok()] * 4
            v_done = sSC.tok()

            def score_mm(t, tt_, hp, c):
                t.matmul(psA[:, (tt_ % 2) * 512:(tt_ % 2) * 512 + 512],
                         khp[0:64, tt_ * 128:(tt_ + 1) * 128],
                         qhp[0:64, c * 512:(c + 1) * 512],
                         start=True, stop=True)
                return t.matmul(psB[:, (tt_ % 2) * 512:(tt_ % 2) * 512 + 512],
                                khp[64:128, tt_ * 128:(tt_ + 1) * 128],
                                qhp[64:128, c * 512:(c + 1) * 512],
                                start=True, stop=True, tile_position=(64, 0))

            def emit_av_pair(t, ta, tb, hp, ntl):
                for tt_ in (ta, tb):
                    t.matmul(psC[0:64, 0:512],
                             vs[:, tt_ * 1024 + (2 * hp) * 64: tt_ * 1024 + (2 * hp) * 64 + 64],
                             p0[:, tt_ * 512:(tt_ + 1) * 512],
                             start=(tt_ == 0), stop=(tt_ == ntl - 1), tile_position=(0, 0))
                    t.matmul(psC[64:128, 512:1024],
                             vs[:, tt_ * 1024 + (2 * hp + 1) * 64: tt_ * 1024 + (2 * hp + 1) * 64 + 64],
                             p1[:, tt_ * 512:(tt_ + 1) * 512],
                             start=(tt_ == 0), stop=(tt_ == ntl - 1), tile_position=(0, 64))
                ins = None
                for tt_ in (ta, tb):
                    t.matmul(psD[0:64, 0:512], ones_bf[:, 0:64], p0[:, tt_ * 512:(tt_ + 1) * 512],
                             start=(tt_ == 0), stop=(tt_ == ntl - 1), tile_position=(0, 0))
                    ins = t.matmul(psD[64:128, 512:1024], ones_bf[:, 0:64],
                                   p1[:, tt_ * 512:(tt_ + 1) * 512],
                                   start=(tt_ == 0), stop=(tt_ == ntl - 1), tile_position=(0, 64))
                return ins

            for hp in range(8):
                pk = [psC[:, 0:512], psC[:, 512:1024], psD[:, 0:512], psD[:, 512:1024]]
                def _(g, hp=hp):
                    slot_dma(g, sWba[0], [st['wba_free'][0]], wba[0][:, :], wqd[hp])
                    slot_dma(g, sWba[1], [st['wba_free'][1]], wba[1][:, :], wkd[hp])
                blk.gpsimd(_)
                wtq, wtk = sWba[0].tok(), sWba[1].tok()

                peq = sPE.n
                def _(t, hp=hp):
                    wait_toks(t, [wtq] + st['bank_pj'])
                    for ic in range(8):
                        for nn in range(4):
                            if hp == 0:
                                wait_toks(t, [(sDV, st['ln_norm_dv'][nn] + 2 * ic + 2)])
                            elif ic == 0 and nn == 0:
                                wait_toks(t, [xln_tok])
                            ins = t.matmul(pj[nn], wba[0][:, ic * 128:(ic + 1) * 128],
                                           act[:, ic * N + nn * 512: ic * N + nn * 512 + 512],
                                           start=(ic == 0), stop=(ic == 7))
                            if ic == 7:
                                winc(sPE, ins)
                    wait_toks(t, [wtk] + st['bank_pk'])
                    for ic in range(8):
                        for nn in range(4):
                            ins = t.matmul(pk[nn], wba[1][:, ic * 128:(ic + 1) * 128],
                                           act[:, ic * N + nn * 512: ic * N + nn * 512 + 512],
                                           start=(ic == 0), stop=(ic == 7))
                            if ic == 7:
                                winc(sPE, ins)
                blk.tensor(_)
                pek = peq + 4
                st['wba_free'] = [sPE.tok(), sPE.tok()]

                dvq = sDV.n
                def _(v):
                    wait_toks(v, [st['uv_free'], t12_tok])
                    for nn in range(4):
                        v.wait_ge(sPE.h, peq + nn + 1)
                        v.tensor_mul(u_sb[:, nn * 512:(nn + 1) * 512], pj[nn],
                                     t1[:, nn * 512:(nn + 1) * 512])
                        winc(sDV, v.tensor_mul(v_sb[:, nn * 512:(nn + 1) * 512], pj[nn],
                                               t2[:, nn * 512:(nn + 1) * 512]))
                    wait_toks(v, [st['uv2_free']])
                    for nn in range(4):
                        v.wait_ge(sPE.h, pek + nn + 1)
                        v.tensor_mul(u2_sb[:, nn * 512:(nn + 1) * 512], pk[nn],
                                     t1[:, nn * 512:(nn + 1) * 512])
                        winc(sDV, v.tensor_mul(v2_sb[:, nn * 512:(nn + 1) * 512], pk[nn],
                                               t2[:, nn * 512:(nn + 1) * 512]))
                blk.vector(_)
                dvk = dvq + 4

                rpq = sPE.n
                def _(t):
                    wait_toks(t, [consts_w])
                    for nn in range(4):
                        t.wait_ge(sDV.h, dvq + nn + 1)
                        t.matmul(pj[nn], shA[:, :], u_sb[:, nn * 512:(nn + 1) * 512],
                                 start=True, stop=False)
                        winc(sPE, t.matmul(pj[nn], shB[:, :], v_sb[:, nn * 512:(nn + 1) * 512],
                                           start=False, stop=True))
                    for nn in range(4):
                        t.wait_ge(sDV.h, dvk + nn + 1)
                        t.matmul(pk[nn], shA[:, :], u2_sb[:, nn * 512:(nn + 1) * 512],
                                 start=True, stop=False)
                        winc(sPE, t.matmul(pk[nn], shB[:, :], v2_sb[:, nn * 512:(nn + 1) * 512],
                                           start=False, stop=True))
                blk.tensor(_)
                rpk = rpq + 4
                st['uv_free'] = (sPE, rpq + 8)
                st['uv2_free'] = (sPE, rpq + 8)

                pe0 = sPE.n
                smm_pred = {}
                av_pred = {}
                nn_ = pe0
                for c in range(4):
                    npr = (4 * c + 4) // 2
                    for p in range(npr):
                        nn_ += 1
                        smm_pred[(c, p)] = nn_
                    nn_ += 1
                    av_pred[c] = nn_
                pe_end = nn_

                sc0 = sSC.n
                dv0_pred = sDV.n
                etok_pred = {}
                sc_evac_pred = {}
                def _(sc, hp=hp):
                    wait_toks(sc, [st['qk_free']])
                    for nn in range(4):
                        sc.wait_ge(sPE.h, rpq + nn + 1)
                        winc(sSC, sc.copy(qhp[:, nn * 512:(nn + 1) * 512], pj[nn]))
                    for nn in range(4):
                        sc.wait_ge(sPE.h, rpk + nn + 1)
                        winc(sSC, sc.copy(khp[:, nn * 512:(nn + 1) * 512], pk[nn]))
                    for c in range(4):
                        ntl = 4 * c + 4
                        if c >= 1:
                            sc.wait_ge(sDV.h, dv0_pred + 6 * c - 2)
                        for p in range(ntl // 2):
                            t0p, t1p = 2 * p, 2 * p + 1
                            sc.wait_ge(sPE.h, smm_pred[(c, p)])
                            if t0p >= 4 * c:
                                for tt_ in (t0p, t1p):
                                    m = tt_ - 4 * c
                                    sc.activation(out=p0[:, tt_ * 512 + 128 * m: (tt_ + 1) * 512],
                                                  in_=psA[:, (tt_ % 2) * 512 + 128 * m: (tt_ % 2) * 512 + 512],
                                                  func=AF.Exp, scale=0.125)
                                    winc(sSC, sc.activation(
                                        out=p1[:, tt_ * 512 + 128 * m: (tt_ + 1) * 512],
                                        in_=psB[:, (tt_ % 2) * 512 + 128 * m: (tt_ % 2) * 512 + 512],
                                        func=AF.Exp, scale=0.125))
                                    etok_pred[(c, tt_)] = (sSC, sSC.n)
                            else:
                                sc.activation(out=p0[:, t0p * 512:(t0p + 2) * 512], in_=psA[:, :],
                                              func=AF.Exp, scale=0.125)
                                winc(sSC, sc.activation(out=p1[:, t0p * 512:(t0p + 2) * 512],
                                                        in_=psB[:, :], func=AF.Exp, scale=0.125))
                                etok_pred[(c, t0p)] = etok_pred[(c, t1p)] = (sSC, sSC.n)
                        sc.wait_ge(sPE.h, av_pred[c])
                        if c >= 1:
                            sc.wait_ge(sDV.h, dv0_pred + 6 * c)
                        winc(sSC, sc.activation(out=scr2[64:128, :], in_=psD[64:128, 512:1024],
                                                func=AF.Ln))
                        sc.wait_ge(sSC.h, sSC.n)
                        winc(sSC, sc.activation(out=scr2[64:128, :], in_=scr2[64:128, :],
                                                func=AF.Exp, scale=-1.0))
                        sc_evac_pred[c] = sSC.n
                blk.scalar(_)

                dv0 = sDV.n
                assert dv0 == dv0_pred + 0 or True
                mtok_pred = {}
                ev_pred = {}
                def _(v, hp=hp):
                    for c in range(4):
                        ntl = 4 * c + 4
                        for tt_ in range(4 * c, ntl):
                            m = tt_ - 4 * c
                            wait_toks(v, [etok_pred[(c, tt_)], consts_w])
                            ins = None
                            for pp2 in (p0, p1):
                                dsl = pp2[:, tt_ * 512 + 128 * m: tt_ * 512 + 128 * m + 128]
                                ins = v.tensor_mul(dsl, dsl, tri[:, :])
                                if m > 0:
                                    ins = v.memset(pp2[:, tt_ * 512: tt_ * 512 + 128 * m], 0.0)
                            winc(sDV, ins)
                            mtok_pred[(c, tt_)] = (sDV, sDV.n)
                        v.wait_ge(sPE.h, av_pred[c])
                        wait_toks(v, [st['act_free']])
                        v.wait_ge(sDV.h, sDV.n)
                        winc(sDV, v.reciprocal(scr2[0:64, :], psD[0:64, 0:512]))
                        v.wait_ge(sDV.h, sDV.n)
                        v.wait_ge(sSC.h, sc_evac_pred[c])
                        v.tensor_mul(att[0:64, hp * N + c * 512: hp * N + c * 512 + 512],
                                     psC[0:64, 0:512], scr2[0:64, :])
                        winc(sDV, v.tensor_mul(att[64:128, hp * N + c * 512: hp * N + c * 512 + 512],
                                               psC[64:128, 512:1024], scr2[64:128, :]))
                        ev_pred[c] = (sDV, sDV.n)
                blk.vector(_)
                assert sDV.n == dv0 + 24, (sDV.n, dv0)

                def _(t, hp=hp):
                    first = True
                    for c in range(4):
                        ntl = 4 * c + 4
                        npr = ntl // 2
                        for p in range(npr + 1):
                            if p < npr:
                                t0p, t1p = 2 * p, 2 * p + 1
                                w = []
                                if first:
                                    w = [v_done, consts_dv, (sSC, sc0 + 8)]
                                    first = False
                                if p >= 1:
                                    w.append(etok_pred[(c, 2 * p - 2)])
                                    w.append(etok_pred[(c, 2 * p - 1)])
                                    if 2 * p - 1 >= 4 * c:
                                        w.append(mtok_pred[(c, 2 * p - 2)])
                                        w.append(mtok_pred[(c, 2 * p - 1)])
                                elif c > 0:
                                    w.append(etok_pred[(c - 1, 4 * c - 1)])
                                    w.append(mtok_pred[(c - 1, 4 * c - 1)])
                                wait_toks(t, w)
                                score_mm(t, t0p, hp, c)
                                winc(sPE, score_mm(t, t1p, hp, c))
                            av_p = p - 1 if p <= npr - 1 else npr - 1
                            if av_p >= 0:
                                ta, tb = 2 * av_p, 2 * av_p + 1
                                w = [etok_pred[(c, ta)], etok_pred[(c, tb)]]
                                if ta >= 4 * c:
                                    w += [mtok_pred[(c, ta)], mtok_pred[(c, tb)]]
                                if av_p == 0:
                                    w += st['bank_pk'] + [v_done]
                                    if c > 0:
                                        w.append(ev_pred[c - 1])
                                wait_toks(t, w)
                                ins = emit_av_pair(t, ta, tb, hp, ntl)
                                if tb == ntl - 1:
                                    winc(sPE, ins)
                blk.tensor(_)
                assert sPE.n == pe_end, (sPE.n, pe_end)
                st['p_free'] = sPE.tok()
                st['bank_pj'] = [(sSC, sc_evac_pred[3])] * 4
                st['bank_pk'] = [ev_pred[3]]
                st['qk_free'] = sPE.tok()
            attn_tok = sDV.tok()

            for ot in range(8):
                px = pj if ot % 2 == 0 else [psC[:, 0:512], psC[:, 512:1024],
                                             psD[:, 0:512], psD[:, 512:1024]]
                def _(g, ot=ot):
                    slot_dma(g, sWba[ot % 2], [st['wba_free'][ot % 2]],
                             wba[ot % 2][:, :], wod[ot])
                blk.gpsimd(_)
                wtok = sWba[ot % 2].tok()

                def _(t, ot=ot, wtok=wtok, px=px):
                    w = [wtok, attn_tok]
                    w += st['bank_pj'] if ot % 2 == 0 else st['bank_pk']
                    wait_toks(t, w)
                    ins = None
                    for ic in range(8):
                        for nn in range(4):
                            ins = t.matmul(px[nn], wba[ot % 2][:, ic * 128:(ic + 1) * 128],
                                           att[:, ic * N + nn * 512: ic * N + nn * 512 + 512],
                                           start=(ic == 0), stop=(ic == 7))
                    winc(sPE, ins)
                blk.tensor(_)
                st['wba_free'][ot % 2] = sPE.tok()
                wo_mm = sPE.tok()

                for nn in range(4):
                    def _(sy, b=b, ot=ot, nn=nn):
                        slot_dma(sy, sLxr[nn % 2], [st['xre_free'][nn % 2]],
                                 xre[nn % 2][:, :],
                                 xT[b, ot * 128:(ot + 1) * 128, nn * 512:(nn + 1) * 512])
                    blk.sync(_)
                    xrl = sLxr[nn % 2].tok()

                    def _(v, nn=nn, wo_mm=wo_mm, xrl=xrl, px=px):
                        wait_toks(v, [wo_mm, xrl])
                        winc(sDV, v.tensor_add(xre[nn % 2][:, :], px[nn], xre[nn % 2][:, :]))
                    blk.vector(_)
                    x1v = sDV.tok()
                    if nn == 3:
                        if ot % 2 == 0:
                            st['bank_pj'] = [sDV.tok()] * 4
                        else:
                            st['bank_pk'] = [sDV.tok()]

                    def _(sy, b=b, ot=ot, nn=nn, x1v=x1v):
                        wait_toks(sy, [x1v, sSxr[nn % 2].tok()])
                        winc(sSxr[nn % 2], sy.dma_start(
                            out=x1s[b, ot * 128:(ot + 1) * 128, nn * 512:(nn + 1) * 512],
                            in_=xre[nn % 2][:, :]))
                        if mode == "attn_only":
                            winc(sSxr[nn % 2], sy.dma_start(
                                out=y[b, ot * 128:(ot + 1) * 128, nn * 512:(nn + 1) * 512],
                                in_=xre[nn % 2][:, :]))
                    blk.sync(_)
                    st['xre_free'][nn % 2] = sSxr[nn % 2].tok()

            st[('attn_done', b)] = [sSxr[0].tok(), sSxr[1].tok()]
            st['act_free'] = sPE.tok()
            st['vs_free'] = sPE.tok()
            st['p_free'] = sPE.tok()
        aes.close()

    def build_ffn():
        fes = contextlib.ExitStack()
        fsb = lambda n_, sh_, dt: fes.enter_context(nc.sbuf_tensor(n_, sh_, dt))
        xln2 = fsb("xln2", [128, 8 * 1024], BF16)
        hA = fsb("hA", [128, 32 * 1024], BF16)
        hB = fsb("hB", [128, 32 * 1024], BF16)
        silu = [fsb(f"silu{i}", [128, 1024], BF16) for i in range(2)]
        w1b = [fsb(f"w1b{i}", [128, 1024], BF16) for i in range(2)]
        w2b = [fsb(f"w2b{i}", [128, 1024], BF16) for i in range(2)]
        hwb = [fsb(f"hwb{i}", [128, 4096], BF16) for i in range(2)]
        x1l = [fsb(f"x1l{i}", [128, 512], F32) for i in range(2)]
        x1w = [fsb(f"x1w{i}", [128, 1024], F32) for i in range(2)]
        sq2 = [fsb(f"sq2{i}", [128, 512], BF16) for i in range(2)]
        xb2 = [fsb(f"xb2{i}", [128, 512], BF16) for i in range(4)]
        fscr = fsb("fscr", [128, 512], F32)

        x1src = xT if mode == "ffn_only" else x1s
        pairs = [psA, psB, psC, psD]

        if mode != "ffn_only":
            ALL_A = ([s.tok() for s in (sLt, sWv, sDV, sSC, sPE)]
                     + [s.tok() for s in sLxr + sWba + sSxr + sGx])
            pinit = list(ALL_A)
            aend_w = list(ALL_A)
        else:
            ALL_A = []
            aend_w = []
            pinit = []
        st.update(f_x1l_free=[ALL_A, ALL_A] if mode != "ffn_only" else [None, None],
                  f_xb_free=[ALL_A] * 4 if mode != "ffn_only" else [None] * 4,
                  f_stats_free=list(ALL_A),
                  f_w1b=[aend_w, aend_w], f_w2b=[aend_w, aend_w],
                  f_hwb=[aend_w, aend_w],
                  f_silu=[ALL_A, ALL_A] if mode != "ffn_only" else [None, None],
                  f_pair_free=[list(pinit) for _i in range(4)],
                  f_hA_free=[consts_ld] + ALL_A, f_hB_free=ALL_A,
                  f_x1w_free=[ALL_A, ALL_A] if mode != "ffn_only" else [None, None],
                  f_out_free=[[consts_ld] + ALL_A, [consts_ld] + ALL_A],
                  f_xln_free=ALL_A if mode != "ffn_only" else None)

        for nb in range(BPC * 2):
            b, nbo = nb // 2, nb % 2
            base_tok = list(ALL_A) if mode != "ffn_only" else []

            f_pend = [None]

            def ln2_flush():
                nsb, bcp, inv_tok, cons2, roff = f_pend[0]
                bc_tok = emit_ln_bc(bcp, inv_tok,
                                    st['f_pair_free'][nsb], roff)
                xldt2 = []
                dv_base = sDV.n
                def _(sy, nsb=nsb):
                    for ic in range(8):
                        g_ = list(cons2[6 + (ic % 2)]) + _aslist(st['f_x1l_free'][ic % 2])
                        if ic >= 2:
                            g_.append((sDV, dv_base + 2 * ic - 2))
                        slot_dma(sy, sLx1l[ic % 2], g_,
                                 x1l[ic % 2][:, :],
                                 x1src[b, ic * 128:(ic + 1) * 128,
                                       nbo * 1024 + nsb * 512: nbo * 1024 + nsb * 512 + 512])
                        xldt2.append(sLx1l[ic % 2].tok())
                blk.sync(_)

                norm_dv = sDV.n
                st.setdefault('f_norm_dv', {})[nsb] = norm_dv
                def _(v, nsb=nsb, bcp=bcp):
                    wait_toks(v, [bc_tok] + _aslist(st['f_xln_free']))
                    for ic in range(8):
                        wait_toks(v, [xldt2[ic]])
                        v.wait_ge(sDV.h, sDV.n)
                        winc(sDV, v.tensor_sub(fscr[:, :], x1l[ic % 2][:, :], bcp[:, 0:512]))
                        v.wait_ge(sDV.h, sDV.n)
                        winc(sDV, v.tensor_mul(
                            xln2[:, ic * 1024 + nsb * 512: ic * 1024 + nsb * 512 + 512],
                            fscr[:, :], bcp[:, 512:1024]))
                blk.vector(_)
                st['f_x1l_free'] = [sDV.tok(), sDV.tok()]
                st['f_pair_free'][nsb] = [sDV.tok()]
                f_pend[0] = None

            for nsb in range(2):
                stp = psC if nsb == 0 else psD
                bcp = psA if nsb == 0 else psB
                xldt = []
                pe_base0 = sPE.n
                dv_base0 = sDV.n
                def _(g, nsb=nsb):
                    for ic in range(8):
                        g_ = _aslist(st['f_xb_free'][ic % 4]) + base_tok
                        if ic >= 4:
                            g_ += [(sPE, pe_base0 + ic - 3), (sDV, dv_base0 + ic - 3)]
                        slot_dma(g, sGx[ic % 4], g_,
                                 xb2[ic % 4][:, :],
                                 x1src[b, ic * 128:(ic + 1) * 128,
                                       nbo * 1024 + nsb * 512: nbo * 1024 + nsb * 512 + 512])
                        xldt.append(sGx[ic % 4].tok())
                blk.gpsimd(_)

                inv_tok, rows_tok, cons2 = emit_ln_stats(
                    lambda ic: xb2[ic % 4][:, :], sq2, stp,
                    st['f_stats_free'] + st['f_pair_free'][2 + nsb], [],
                    xldt, nsb * 1024)
                st['f_stats_free'] = [rows_tok]
                st['f_pair_free'][2 + nsb] = [rows_tok]
                st['f_xb_free'] = [cons2[4], cons2[5], cons2[6], cons2[7]]

                if f_pend[0] is not None:
                    ln2_flush()
                f_pend[0] = (nsb, bcp, inv_tok, cons2, nsb * 1024)
            ln2_flush()
            xln2_tok = sDV.tok()

            g1_base = [sW1[0].n, sW1[1].n]
            g2_base = [sW2[0].n, sW2[1].n]
            pe_b = sPE.n
            sc_b = sSC.n
            dv_b = sDV.n
            def _(g):
                for ot in range(32):
                    g_ = _aslist(st['f_w1b'][ot % 2]) if ot < 2 else [(sPE, pe_b + 2 * ot - 3)]
                    slot_dma(g, sW1[ot % 2], g_, w1b[ot % 2][:, :], w1d[ot])
                    g_ = _aslist(st['f_w2b'][ot % 2]) if ot < 2 else [(sPE, pe_b + 2 * ot - 2)]
                    slot_dma(g, sW2[ot % 2], g_, w2b[ot % 2][:, :], w2d[ot])
            blk.gpsimd(_)

            def _(t):
                for ot in range(32):
                    pw1 = pairs[(ot % 2) * 2]
                    pw2 = pairs[(ot % 2) * 2 + 1]
                    w = [(sW1[ot % 2], g1_base[ot % 2] + 16 * (ot // 2 + 1)),
                         (sW2[ot % 2], g2_base[ot % 2] + 16 * (ot // 2 + 1))]
                    if ot == 1:
                        w.append(xln2_tok)
                    if ot < 2:
                        w += st['f_pair_free'][(ot % 2) * 2] + st['f_pair_free'][(ot % 2) * 2 + 1]
                    else:
                        w += [(sSC, sc_b + ot - 1), (sDV, dv_b + ot - 1)]
                    wait_toks(t, w)
                    for wb_, ps_ in ((w1b, pw1), (w2b, pw2)):
                        ins = None
                        for ic in range(8):
                            for nsb in range(2):
                                if ot == 0 and wb_ is w1b:
                                    wait_toks(t, [(sDV, st['f_norm_dv'][nsb] + 2 * ic + 2)])
                                ins = t.matmul(ps_[:, nsb * 512:(nsb + 1) * 512],
                                               wb_[ot % 2][:, ic * 128:(ic + 1) * 128],
                                               xln2[:, ic * 1024 + nsb * 512: ic * 1024 + nsb * 512 + 512],
                                               start=(ic == 0), stop=(ic == 7))
                        winc(sPE, ins)
            blk.tensor(_)

            def _(sc):
                for ot in range(32):
                    pw1 = pairs[(ot % 2) * 2]
                    w = [(sPE, pe_b + 2 * ot + 1)]
                    if ot < 2:
                        w += _aslist(st['f_silu'][ot % 2])
                    else:
                        w.append((sDV, dv_b + ot - 1))
                    wait_toks(sc, w)
                    winc(sSC, sc.activation(out=silu[ot % 2][:, :], in_=pw1[:, :], func=AF.Silu,
                                            scale=s1v[:, ot:ot + 1]))
            blk.scalar(_)

            def _(v):
                for ot in range(32):
                    pw2 = pairs[(ot % 2) * 2 + 1]
                    w = [(sPE, pe_b + 2 * ot + 2), (sSC, sc_b + ot + 1)]
                    if ot == 0:
                        w += _aslist(st['f_hA_free'])
                    wait_toks(v, w)
                    winc(sDV, v.scalar_tensor_tensor(
                        hA[:, ot * 1024:(ot + 1) * 1024], pw2[:, :], s2v[:, ot:ot + 1],
                        silu[ot % 2][:, :], ALU.mult, ALU.mult))
            blk.vector(_)
            st['f_w1b'] = [(sPE, pe_b + 61), (sPE, pe_b + 63)]
            st['f_w2b'] = [(sPE, pe_b + 62), (sPE, pe_b + 64)]
            st['f_silu'] = [(sDV, dv_b + 31), (sDV, dv_b + 32)]
            st['f_pair_free'] = [[(sSC, sc_b + 31)], [(sDV, dv_b + 31)],
                                 [(sSC, sc_b + 32)], [(sDV, dv_b + 32)]]
            hA_tok = sDV.tok()

            for li, (wsrc, scl, src_, dst_) in enumerate([
                    (h1d, sh1v, hA, hB), (h2d, sh2v, hB, hA)]):
                gh_base = [sWh[0].n, sWh[1].n]
                pe_b = sPE.n
                sc_b = sSC.n
                def _(g, wsrc=wsrc, pe_b=pe_b):
                    for ot in range(32):
                        g_ = (_aslist(st['f_hwb'][ot % 2]) if ot < 2
                              else [(sPE, pe_b + ot - 1)])
                        slot_dma(g, sWh[ot % 2], g_, hwb[ot % 2][:, :], wsrc[ot])
                blk.gpsimd(_)

                def _(t, li=li, src_=src_, pe_b=pe_b, sc_b=sc_b, gh_base=gh_base):
                    for ot in range(32):
                        ps_ = pairs[ot % 2]
                        w = [(sWh[ot % 2], gh_base[ot % 2] + 16 * (ot // 2 + 1))]
                        if ot < 2:
                            w += st['f_pair_free'][ot % 2]
                            w.append(hA_tok if li == 0 else l1_all)
                        else:
                            w.append((sSC, sc_b + ot - 1))
                        wait_toks(t, w)
                        ins = None
                        for ic in range(32):
                            for nsb in range(2):
                                ins = t.matmul(ps_[:, nsb * 512:(nsb + 1) * 512],
                                               hwb[ot % 2][:, ic * 128:(ic + 1) * 128],
                                               src_[:, ic * 1024 + nsb * 512: ic * 1024 + nsb * 512 + 512],
                                               start=(ic == 0), stop=(ic == 31))
                        winc(sPE, ins)
                blk.tensor(_)

                def _(sc, li=li, dst_=dst_, scl=scl, pe_b=pe_b):
                    for ot in range(32):
                        ps_ = pairs[ot % 2]
                        w = [(sPE, pe_b + ot + 1)]
                        if ot == 0:
                            w += _aslist(st['f_hB_free'] if li == 0 else st['f_hA2_free'])
                        wait_toks(sc, w)
                        winc(sSC, sc.activation(out=dst_[:, ot * 1024:(ot + 1) * 1024],
                                                in_=ps_[:, :], func=AF.Gelu,
                                                scale=scl[:, ot:ot + 1]))
                blk.scalar(_)
                st['f_hwb'] = [(sPE, pe_b + 31), (sPE, pe_b + 32)]
                st['f_pair_free'][0] = [(sSC, sc_b + 31)]
                st['f_pair_free'][1] = [(sSC, sc_b + 32)]
                if li == 0:
                    l1_all = sSC.tok()
                    st['f_hA2_free'] = None
            hh_tok = sSC.tok()
            st['f_hB_free'] = None

            gh_base = [sWh[0].n, sWh[1].n]
            pe_b = sPE.n
            dv_b = sDV.n
            def _(g):
                for ot in range(8):
                    g_ = (_aslist(st['f_hwb'][ot % 2]) if ot < 2
                          else [(sPE, pe_b + ot - 1)])
                    slot_dma(g, sWh[ot % 2], g_, hwb[ot % 2][:, :], w3d[ot])
            blk.gpsimd(_)

            x1w_base = [sLx1w[0].n, sLx1w[1].n]
            so_base = [sSo[0].n, sSo[1].n]
            def _(sy):
                for ot in range(10):
                    if ot >= 2:
                        k = ot - 2
                        wait_toks(sy, [(sDV, dv_b + k + 1), sSo[k % 2].tok()])
                        winc(sSo[k % 2], sy.dma_start(
                            out=y[b, k * 128:(k + 1) * 128, nbo * 1024:(nbo + 1) * 1024],
                            in_=x1w[k % 2][:, :]))
                    if ot < 8:
                        g_ = list(base_tok)
                        if ot < 2:
                            g_ += _aslist(st['f_x1w_free'][ot % 2])
                        else:
                            g_ += [(sDV, dv_b + ot - 1),
                                   (sSo[ot % 2], so_base[ot % 2] + 16 * (ot // 2))]
                        slot_dma(sy, sLx1w[ot % 2], g_,
                                 x1w[ot % 2][:, :],
                                 x1src[b, ot * 128:(ot + 1) * 128, nbo * 1024:(nbo + 1) * 1024])
            blk.sync(_)

            def _(t, pe_b=pe_b, dv_b=dv_b, gh_base=gh_base):
                for ot in range(8):
                    ps_ = pairs[ot % 2]
                    w = [(sWh[ot % 2], gh_base[ot % 2] + 16 * (ot // 2 + 1)), hh_tok]
                    if ot < 2:
                        w += st['f_pair_free'][ot % 2]
                    else:
                        w.append((sDV, dv_b + ot - 1))
                    wait_toks(t, w)
                    ins = None
                    for ic in range(32):
                        for nsb in range(2):
                            ins = t.matmul(ps_[:, nsb * 512:(nsb + 1) * 512],
                                           hwb[ot % 2][:, ic * 128:(ic + 1) * 128],
                                           hA[:, ic * 1024 + nsb * 512: ic * 1024 + nsb * 512 + 512],
                                           start=(ic == 0), stop=(ic == 31))
                    winc(sPE, ins)
            blk.tensor(_)

            def _(v, pe_b=pe_b):
                for ot in range(8):
                    ps_ = pairs[ot % 2]
                    w = [(sPE, pe_b + ot + 1),
                         (sLx1w[ot % 2], x1w_base[ot % 2] + 16 * (ot // 2 + 1))]
                    if ot < 2:
                        w += _aslist(st['f_out_free'][ot % 2])
                    else:
                        w.append((sSo[ot % 2], so_base[ot % 2] + 16 * (ot // 2)))
                    wait_toks(v, w)
                    winc(sDV, v.scalar_tensor_tensor(
                        x1w[ot % 2][:, :], ps_[:, :], s3v[:, ot:ot + 1],
                        x1w[ot % 2][:, :], ALU.mult, ALU.add))
            blk.vector(_)

            st['f_hwb'] = [(sPE, pe_b + 7), (sPE, pe_b + 8)]
            st['f_x1w_free'] = [[(sDV, dv_b + 7), (sSo[0], so_base[0] + 64)],
                                [(sDV, dv_b + 8), (sSo[1], so_base[1] + 64)]]
            st['f_pair_free'][0] = [(sDV, dv_b + 7)]
            st['f_pair_free'][1] = [(sDV, dv_b + 8)]
            st['f_out_free'] = [sSo[0].tok(), sSo[1].tok()]

            nb_end = sPE.tok()
            st['f_hA_free'] = nb_end
            st['f_hB_free'] = nb_end
            st['f_hA2_free'] = None
            st['f_xln_free'] = nb_end
        fes.close()

    if mode != "ffn_only":
        build_attention()
    if mode != "attn_only":
        build_ffn()

    def _(sy):
        wait_toks(sy, [sSxr[0].tok(), sSxr[1].tok(), sSo[0].tok(), sSo[1].tok()])
    blk.sync(_)

    es.close()
    return nc


def _rope_perm():
    perm = []
    for t in range(8):
        for hh in (2 * t, 2 * t + 1):
            perm += [hh * 64 + 2 * j for j in range(32)]
            perm += [hh * 64 + 2 * j + 1 for j in range(32)]
    return np.array(perm)


def _block_w(wt, not_):
    d_in, d_out = wt.shape
    nic = d_in // 128
    a = wt.reshape(nic, 128, not_, 128)
    return np.ascontiguousarray(a.transpose(2, 1, 0, 3).reshape(not_, 128, nic * 128))


def _tables():
    import jax
    import jax.numpy as jnp
    with jax.default_device(jax.devices('cpu')[0]):
        exponent = jnp.arange(32, dtype=jnp.float32) * (2.0 / 64)
        inv_freq = 10000.0 ** (-exponent)
        pos = jnp.arange(N, dtype=jnp.float32)
        ang = pos[:, None] * inv_freq[None, :]
        cos = np.ascontiguousarray(np.asarray(jnp.cos(ang)).T)
        sin = np.ascontiguousarray(np.asarray(jnp.sin(ang)).T)
    t1 = np.concatenate([cos, sin, cos, sin], 0)
    t2 = np.concatenate([sin, cos, sin, cos], 0)
    A = np.zeros((128, 128), np.float32)
    Bm = np.zeros((128, 128), np.float32)
    for blk0 in (0, 64):
        for jj in range(32):
            m1, m2 = blk0 + jj, blk0 + 32 + jj
            A[m1, m1] = 1.0
            A[m2, m1] = -1.0
            Bm[m1, m2] = 1.0
            Bm[m2, m2] = 1.0
    tri_ = np.zeros((128, 128), np.float32)
    for kk in range(128):
        tri_[kk, kk:] = 1.0
    return t1, t2, np.ascontiguousarray(np.concatenate([A, Bm], 1)), tri_


_prog_cache = {}


def kernel(L0_attn_x, L0_attn_Wq, L0_attn_Wk, L0_attn_Wv, L0_attn_Wo,
           L0_ffn_w1_w, L0_ffn_w1_s, L0_ffn_w2_w, L0_ffn_w2_s,
           L0_ffn_h1_w, L0_ffn_h1_s, L0_ffn_h2_w, L0_ffn_h2_s,
           L0_ffn_w3_w, L0_ffn_w3_s):
    mode = _CFG["mode"]
    x = np.ascontiguousarray(np.asarray(L0_attn_x, dtype=np.float32))
    perm = _rope_perm()
    wq_t = np.asarray(L0_attn_Wq, np.float32).T[:, perm]
    wk_t = np.asarray(L0_attn_Wk, np.float32).T[:, perm]
    wv_t = np.asarray(L0_attn_Wv, np.float32).T
    wo_t = np.asarray(L0_attn_Wo, np.float32).T
    w1_t = np.asarray(L0_ffn_w1_w, np.float32).T
    w2_t = np.asarray(L0_ffn_w2_w, np.float32).T
    h1_t = np.asarray(L0_ffn_h1_w, np.float32).T
    h2_t = np.asarray(L0_ffn_h2_w, np.float32).T
    w3_t = np.asarray(L0_ffn_w3_w, np.float32).T

    wv_r = np.ascontiguousarray(
        wv_t.reshape(8, 128, 2, 512).transpose(2, 1, 0, 3).reshape(2, 128, 4096))
    t1, t2, sh, tri_ = _tables()
    shared = {
        "wq": _block_w(wq_t, 8), "wk": _block_w(wk_t, 8), "wv": wv_r,
        "wo": _block_w(wo_t, 8),
        "w1": _block_w(w1_t, 32), "w2": _block_w(w2_t, 32),
        "h1w": _block_w(h1_t, 32), "h2w": _block_w(h2_t, 32),
        "w3": _block_w(w3_t, 8),
        "s1": np.ascontiguousarray(np.asarray(L0_ffn_w1_s, np.float32).reshape(32, 128).T),
        "s2": np.ascontiguousarray(np.asarray(L0_ffn_w2_s, np.float32).reshape(32, 128).T),
        "sh1": np.ascontiguousarray(np.asarray(L0_ffn_h1_s, np.float32).reshape(32, 128).T),
        "sh2": np.ascontiguousarray(np.asarray(L0_ffn_h2_s, np.float32).reshape(32, 128).T),
        "s3": np.ascontiguousarray(np.asarray(L0_ffn_w3_s, np.float32).reshape(8, 128).T),
        "t1": t1, "t2": t2, "sh": sh, "tri": tri_,
    }

    if mode not in _prog_cache:
        _prog_cache[mode] = build_program(mode)
    nc = _prog_cache[mode]

    in_maps = []
    for c in range(NCORE):
        xc = np.ascontiguousarray(x[BPC * c: BPC * (c + 1)].transpose(0, 2, 1))
        in_maps.append({"xT": xc, **shared})

    res = run_bass_kernel_spmd(nc, in_maps, core_ids=list(range(NCORE)))
    out = np.empty((B, N, D), np.float32)
    for c in range(NCORE):
        out[BPC * c: BPC * (c + 1)] = np.asarray(res.results[c]["y"]).transpose(0, 2, 1)
    return out
